# version 01
# baseline (speedup 1.0000x reference)
import contextlib
import math
import numpy as np
import concourse.bass as bass
import concourse.mybir as mybir
from concourse.bass_utils import run_bass_kernel_spmd

F32 = mybir.dt.float32
BF16 = mybir.dt.bfloat16
I32 = mybir.dt.int32
U8 = mybir.dt.uint8
F16 = mybir.dt.float16
AF = mybir.ActivationFunctionType
ALU = mybir.AluOpType
AX = mybir.AxisListType.X
TS = bass.ts
DS = bass.ds
ET = mybir.EngineType

MAGIC = 1.5 * 2.0 ** 23
LN2 = math.log(2.0)

NCORE = 8
NT = 4096
NHD = 16
D = 2048
DI = 5632
NDC = D // 128
NIC = DI // 128


class Eng:

    def __init__(self, pb, engine, name, step=1):
        self.pb = pb
        self.e = engine
        self.name = name
        self.step = step
        self.s = pb.es.enter_context(pb.nc.semaphore(name + "_sem"))
        self.base = 0
        self.stack = []

    def inc(self, inst):
        inst.then_inc(self.s, self.step)
        self._bump(self.step)
        return inst

    def _bump(self, n):
        if self.stack:
            self.stack[-1][1] += n
            self.stack[-1][2] += n
        else:
            self.base += n

    def ktot(self):
        return sum(kr for (_, kt, kr) in self.stack)

    def wait(self, other, back=0):
        pb = self.pb
        if not pb.loopstack:
            self.e.wait_ge(other.s, other.base - back)
            return
        pair = (self.name, other.name)
        pb.all_pairs.add(pair)
        kconst = other.ktot() - back
        assert kconst + other.base >= 0
        rb = pb.pair_reg(self, other)
        rt = pb.tmp_reg(self)
        self.e.reg_alu(rt, rb, kconst, op=ALU.add)
        self.e.wait_ge(other.s, rt)

    def chain(self):
        self.wait(self)

    def wait_of(self, eng_other):
        self.wait(eng_other)

    def drain(self):
        self.wait(self)


class IdxCounter:
    def __init__(self, pb, q, name, hi):
        self.pb, self.q = pb, q
        self.step, self.wrap, self.cap = 1, None, None
        self.r = pb.reg(q, name)
        self.lo, self.hi = 0, hi
        self._sv = q.e.snap(self.r, min_val=0, max_val=hi, donate=True)

    def reinit(self, init, step, wrap, cap):
        self.step, self.wrap, self.cap = step, wrap, cap
        self.q.e.reg_mov(self.r, init)
        return self

    def use(self):
        return self._sv

    def adv(self):
        e = self.q.e
        e.reg_alu(self.r, self.r, self.step, op=ALU.add)
        if self.wrap is not None:
            e.reg_alu(self.r, self.r, self.wrap, op=ALU.mod)
        elif self.cap is not None:
            e.reg_alu(self.r, self.r, self.cap, op=ALU.min)


class PB:
    def __init__(self, registry=None):
        self.nc = bass.Bass("TRN2", target_bir_lowering=False, debug=False)
        self.es = contextlib.ExitStack()
        self.registry = registry if registry is not None else {}
        self.new_registry = {}
        nc = self.nc
        self.pe = Eng(self, nc.tensor, "pe")
        self.dv = Eng(self, nc.vector, "dv")
        self.ac = Eng(self, nc.scalar, "ac")
        self.gp = Eng(self, nc.gpsimd, "gp")
        self.qs = Eng(self, nc.sync, "qs", step=16)
        self.qsb = Eng(self, nc.sync, "qsb", step=16)
        self.qg = Eng(self, nc.gpsimd, "qg", step=16)
        self.qgb = Eng(self, nc.gpsimd, "qgb", step=16)
        self.engines = [self.pe, self.dv, self.ac, self.gp,
                        self.qs, self.qsb, self.qg, self.qgb]
        self.byname = {e.name: e for e in self.engines}
        self.loopstack = []
        self.all_pairs = set(map(tuple, (registry or {}).get("__pairs__", [])))
        self.ident32 = self.sb("g_ident32", [128, 128], F32)
        self.ps8 = self.psum("g_ps8", [128, 8, 512], F32)
        self.regs = {}
        self.constvals = [-7.5, LN2, -LN2, 1.0, 0.5, -0.5, 2.0]
        self.consts = self.sb("g_consts", [128, len(self.constvals)], F32)
        for k, v in enumerate(self.constvals):
            self.nc.const_aps.register(F32, v, self.consts[:, k:k + 1])

    def reg(self, eng, name):
        key = (eng.name, name)
        if key not in self.regs:
            self.regs[key] = eng.e.alloc_register(name)
        return self.regs[key]

    def sb(self, name, shape, dt=F32):
        return self.es.enter_context(self.nc.sbuf_tensor(name, shape, dt))

    def psum(self, name, shape, dt=F32):
        return self.es.enter_context(self.nc.psum_tensor(name, shape, dt))

    def bar(self):
        for q in (self.qs, self.qsb, self.qg, self.qgb):
            q.drain()
        self.nc.all_engine_barrier()

    @contextlib.contextmanager
    def loop(self, n, key):
        assert n >= 1
        reg = self.registry.get(key, {"deltas": {}})
        rec = {"deltas": {},
               "inner": {e.name: 0 for e in self.engines}}
        if not self.loopstack:
            for (wn, pn) in sorted(self.all_pairs):
                w = self.byname[wn]
                p = self.byname[pn]
                w.e.reg_mov(self.pair_reg(w, p), p.base)
        self.loopstack.append(rec)
        with self.nc.Fori(0, n) as i:
            for e in self.engines:
                e.stack.append([rec, 0, 0])
            yield i
            deltas = {}
            for e in self.engines:
                deltas[e.name] = e.stack[-1][1]
            for (wn, pn) in sorted(self.all_pairs):
                w = self.byname[wn]
                resid = deltas[pn] - rec["inner"][pn]
                if resid != 0:
                    w.e.reg_alu(self.pair_reg(w, self.byname[pn]),
                                self.pair_reg(w, self.byname[pn]),
                                resid, op=ALU.add)
            for e in self.engines:
                _, kt, kr = e.stack.pop()
                if key in self.registry:
                    d = reg["deltas"].get(e.name, 0)
                    assert kt == d, (f"loop {key} delta mismatch {e.name}: "
                                     f"registered {d}, emitted {kt}")
                add = n * kt
                if e.stack:
                    e.stack[-1][1] += add
                else:
                    e.base += add
            self.loopstack.pop()
            if self.loopstack:
                parent = self.loopstack[-1]
                for e in self.engines:
                    parent["inner"][e.name] += n * deltas[e.name]
        self.new_registry[key] = {
            "deltas": {k: v for k, v in deltas.items() if v},
        }
        self.new_registry["__pairs__"] = sorted(self.all_pairs)

    def pair_reg(self, w, p):
        key = ("pair", w.name, p.name)
        if key not in self.regs:
            self.regs[key] = w.e.alloc_register(f"rb_{w.name}_{p.name}")
        return self.regs[key]

    def tmp_reg(self, w):
        key = ("tmp", w.name)
        if key not in self.regs:
            self.regs[key] = w.e.alloc_register(f"rt_{w.name}")
        return self.regs[key]

    def counter(self, q, name, init, step, wrap=None, cap=None, maxv=None):
        hi = maxv if maxv is not None else (
            (wrap - 1) if wrap is not None else cap)
        key = ("ctr", q.name, name, hi)
        if key not in self.regs:
            self.regs[key] = IdxCounter(self, q, name + f"_{hi}", hi)
        return self.regs[key].reinit(init, step, wrap, cap)




def A(x):
    return x if isinstance(x, bass.AP) else x[:]

def setup_common(pb):
    nc, gp = pb.nc, pb.gp
    for k, v in enumerate(pb.constvals):
        gp.inc(nc.gpsimd.memset(pb.consts[:, k:k + 1], v))
    gp.chain()
    gp.inc(nc.gpsimd.memset(pb.ident32[:], 0.0))
    gp.chain()
    gp.inc(nc.gpsimd.affine_select(
        out=pb.ident32[:], in_=pb.ident32[:], compare_op=ALU.not_equal,
        fill=1.0, base=0, pattern=[[-1, 128]], channel_multiplier=1))
    gp.chain()
    pb.bar()




def ln_tile(pb, x_ap, xn_ap, tmp_ap, cols, ci):
    nc, dv, ac = pb.nc, pb.dv, pb.ac
    x_ap, xn_ap, tmp_ap = A(x_ap), A(xn_ap), A(tmp_ap)
    m = cols[:, ci:ci + 1]
    v = cols[:, ci + 1:ci + 2]
    r = cols[:, ci + 2:ci + 3]
    dv.inc(nc.vector.reduce_sum(out=m, in_=x_ap, axis=AX))
    dv.chain()
    dv.inc(nc.vector.tensor_scalar(out=m, in0=m, scalar1=1.0 / D, scalar2=None,
                                   op0=ALU.mult))
    dv.chain()
    dv.inc(nc.vector.tensor_scalar(out=tmp_ap, in0=x_ap, scalar1=m,
                                   scalar2=None, op0=ALU.subtract))
    dv.chain()
    ac.wait(dv)
    ac.inc(nc.scalar.activation(out=xn_ap, in_=tmp_ap, func=AF.Square))
    ac.chain()
    dv.wait(ac)
    dv.inc(nc.vector.reduce_sum(out=v, in_=xn_ap, axis=AX))
    dv.chain()
    dv.inc(nc.vector.tensor_scalar(out=v, in0=v, scalar1=1.0 / D, scalar2=1e-5,
                                   op0=ALU.mult, op1=ALU.add))
    dv.chain()
    ac.wait(dv)
    ac.inc(nc.scalar.activation(out=r, in_=v, func=AF.Sqrt))
    ac.chain()
    dv.wait(ac)
    dv.inc(nc.vector.reciprocal(out=r, in_=r))
    dv.chain()
    dv.inc(nc.vector.tensor_scalar(out=xn_ap, in0=tmp_ap, scalar1=r,
                                   scalar2=None, op0=ALU.mult))
    dv.chain()


def emit_actquant(pb, x_ap, q_ap, cols, ci, inv_qmax=1.0 / 127.0, eps=1e-8):
    nc, dv = pb.nc, pb.dv
    x_ap, q_ap = A(x_ap), A(q_ap)
    am = cols[:, ci:ci + 1]
    sc = cols[:, ci + 1:ci + 2]
    rc = cols[:, ci + 2:ci + 3]
    dv.inc(nc.vector.reduce_max(out=am, in_=x_ap, axis=AX,
                                apply_absolute_value=True))
    dv.chain()
    dv.inc(nc.vector.tensor_scalar(out=sc, in0=am, scalar1=inv_qmax,
                                   scalar2=eps, op0=ALU.mult, op1=ALU.max))
    dv.chain()
    dv.inc(nc.vector.reciprocal(out=rc, in_=sc))
    dv.chain()
    dv.inc(nc.vector.tensor_scalar(out=q_ap, in0=x_ap, scalar1=rc,
                                   scalar2=MAGIC, op0=ALU.mult, op1=ALU.add))
    dv.chain()
    dv.inc(nc.vector.tensor_scalar(out=q_ap, in0=q_ap, scalar1=MAGIC,
                                   scalar2=None, op0=ALU.subtract))
    dv.chain()


def transpose_f32_to_bf16(pb, src_of, dst_of, nchunk, bank):
    nc, pe, dv = pb.nc, pb.pe, pb.dv
    for c0 in range(0, nchunk, 4):
        w = min(4, nchunk - c0)
        pe.wait(dv)
        for c in range(c0, c0 + w):
            tr = nc.tensor.transpose(bank[:, (c - c0) * 128:(c - c0 + 1) * 128],
                                     src_of(c), pb.ident32[:])
            if c == c0 + w - 1:
                pe.inc(tr)
        dv.wait(pe)
        dv.inc(nc.vector.tensor_copy(
            out=dst_of(c0, w),
            in_=bank[:, 0:w * 128].rearrange("p (a b) -> p a b", b=128)))
        dv.chain()




def emit_prep_rowquant(pb, w_ap, wT_ap, O, K, groups, qmax=127.0,
                       eps=1e-8, key="prep", rowquant=True):
    nc = pb.nc
    dv, pe = pb.dv, pb.pe
    qs, qsb, qg = pb.qs, pb.qsb, pb.qg
    KC = K // 128
    w3 = w_ap.rearrange("(n p) k -> n p k", p=128)

    with contextlib.ExitStack() as es2:
        pbuf = es2.enter_context(nc.sbuf_tensor(key + "_pbuf", [128, 2, K], F32))
        wq16 = es2.enter_context(nc.sbuf_tensor(key + "_wq", [128, K], F32))
        tq = es2.enter_context(nc.sbuf_tensor(key + "_tq", [128, KC, 128], BF16))
        ccols = es2.enter_context(nc.sbuf_tensor(key + "_cc", [128, 4], F32))
        bank = pb.ps8[:, 0, :]

        gsz = len(groups[0])
        ngrp = len(groups)
        base = [groups[0][j] for j in range(gsz)]
        step = ([(groups[1][j][0] - groups[0][j][0],
                  groups[1][j][1] - groups[0][j][1]) for j in range(gsz)]
                if ngrp > 1 else [(0, 0)] * gsz)
        for g in range(ngrp):
            for j in range(gsz):
                assert groups[g][j] == (base[j][0] + step[j][0] * g,
                                        base[j][1] + step[j][1] * g), (g, j)

        for j in range(min(gsz, 2)):
            q = qs if j % 2 == 0 else qsb
            q.inc(q.e.dma_start(out=pbuf[:, j % 2, :],
                                in_=w3[base[j][0], :, :]))

        TSZ = 128 * K
        cmax = (O // 128 - 1) * TSZ
        ctr = []
        for j in range(gsz):
            nxt = j + 2
            q = qs if j % 2 == 0 else qsb
            if nxt < gsz:
                ctr.append(pb.counter(q, f"ctr{j}", base[nxt][0] * TSZ,
                                      step[nxt][0] * TSZ, cap=cmax))
            else:
                sl = nxt - gsz
                ctr.append(pb.counter(q, f"ctr{j}",
                                      (base[sl][0] + step[sl][0]) * TSZ,
                                      step[sl][0] * TSZ, cap=cmax))
        dctr = [pb.counter(pb.qg, f"dtr{j}", base[j][1], step[j][1],
                           cap=base[j][1] + step[j][1] * max(ngrp - 1, 1))
                for j in range(gsz)]

        with pb.loop(ngrp, key=key) as g:
            for j in range(gsz):
                par = j % 2
                q = qs if par == 0 else qsb
                dv.wait(q)
                dv.wait(qg)
                dv.wait(pe)
                src = pbuf[:, par, :]
                if rowquant:
                    emit_actquant(pb, src, wq16[:, :], ccols, 0, 1.0 / qmax, eps)
                    dv.inc(nc.vector.tensor_scalar(
                        out=wq16[:, :], in0=wq16[:, :], scalar1=ccols[:, 1:2],
                        scalar2=None, op0=ALU.mult))
                    dv.chain()
                else:
                    dv.inc(nc.vector.tensor_copy(out=wq16[:, :], in_=src))
                    dv.chain()
                q.wait(dv)
                q.wait(q)
                q.inc(q.e.dma_start(
                    out=pbuf[:, par, :],
                    in_=bass.AP(w_ap.tensor, ctr[j].use(),
                                [[K, 128], [1, K]])))
                ctr[j].adv()
                transpose_f32_to_bf16(
                    pb, lambda c: wq16[:, c * 128:(c + 1) * 128],
                    lambda c0, w: tq[:, c0:c0 + w, :],
                    KC, bank)
                qg.wait(dv)
                qg.inc(qg.e.dma_start(
                    out=wT_ap[:, :, DS(dctr[j].use(), 128)],
                    in_=tq[:, :, :]))
                dctr[j].adv()
        pb.bar()


def gu_pair_groups(I=DI, gran=256):
    tpg = gran // 128
    groups = []
    for q in range(I // gran):
        g = []
        for t in range(tpg):
            g.append((q * tpg + t, q * 2 * gran + t * 128))
        for t in range(tpg):
            g.append((I // 128 + q * tpg + t, q * 2 * gran + gran + t * 128))
        groups.append(g)
    return groups


def plain_groups(O):
    return [[(2 * g, 2 * g * 128), (2 * g + 1, (2 * g + 1) * 128)]
            for g in range(O // 256)]




def emit_l1ffn_main(pb, Rin_ap, Rout_ap, w1T_ap, w2T_ap, upto='full', mode='l1'):
    nc = pb.nc
    dv, ac, pe, gp = pb.dv, pb.ac, pb.pe, pb.gp
    qs, qsb, qg, qgb = pb.qs, pb.qsb, pb.qg, pb.qgb
    NSPAN = 22
    NGRP = 16
    ps8 = pb.ps8

    rgrp = pb.sb("f1_rgrp", [128, 2, D], F32)
    tmA = pb.sb("f1_tmA", [128, D], F32)
    tmB = pb.sb("f1_tmB", [128, D], F32)
    e256 = pb.sb("f1_e256", [128, 2, 256], F32)
    u256 = pb.sb("f1_u256", [128, 2, 256], F32)
    xqf = pb.sb("f1_xqf", [128, NDC, 256], BF16)
    wbuf = pb.sb("f1_wbuf", [128, 2, NDC, 512], BF16)
    hbig = pb.sb("f1_h", [128, 2, DI], F32)
    hqf = pb.sb("f1_hqf", [128, NIC, 256], BF16)
    w2buf = pb.sb("f1_w2", [128, 2, D], BF16)
    rres = pb.sb("f1_rres", [128, 2, D], F32)
    ccol = pb.sb("f1_cols", [128, 16], F32)

    Rin3 = Rin_ap.rearrange("(a p) d -> p a d", p=128)
    Rout3 = Rout_ap.rearrange("(a p) d -> p a d", p=128)

    qg.inc(qg.e.dma_start(out=rgrp[:, 0:1, :], in_=Rin3[:, 0:1, :]))
    qg.inc(qg.e.dma_start(out=rgrp[:, 1:2, :], in_=Rin3[:, 1:2, :]))
    qs.inc(qs.e.dma_start(out=wbuf[:, 0, :, :], in_=w1T_ap[:, :, 0:512]))
    qsb.inc(qsb.e.dma_start(out=wbuf[:, 1, :, :], in_=w1T_ap[:, :, 512:1024]))
    qg.inc(qg.e.dma_start(out=w2buf[:, 0, :],
                          in_=w2T_ap[:, 0:1, :].rearrange("p a b -> p (a b)")))
    qgb.inc(qgb.e.dma_start(out=w2buf[:, 1, :],
                            in_=w2T_ap[:, 1:2, :].rearrange("p a b -> p (a b)")))
    wctr = [pb.counter(qs, "ctr0", 2 * 512, 1024, wrap=NSPAN * 512,
                       maxv=(NSPAN - 1) * 512),
            pb.counter(qsb, "ctr1", 3 * 512, 1024, wrap=NSPAN * 512,
                       maxv=(NSPAN - 1) * 512)]
    RT = 128 * D
    rctr = [pb.counter(qg, "dtr0", 2 * RT, 2 * RT, cap=(2 * NGRP - 1) * RT),
            pb.counter(qg, "dtr1", 3 * RT, 2 * RT, cap=(2 * NGRP - 1) * RT)]
    octr = pb.counter(qgb, "dtr2", 0, 2 * RT, cap=(2 * NGRP - 2) * RT)

    with pb.loop(NGRP, key="f1_main") as g:
        dv.wait(qg)
        dv.wait(pe)
        for t in range(2):
            ln_tile(pb, rgrp[:, t, :], tmA[:, :], tmB[:, :], ccol, 0)
            if mode == 'l1':
                emit_actquant(pb, tmA, tmB, ccol, 4 + 4 * t)
            else:
                tmB, tmA = tmA, tmB
            transpose_f32_to_bf16(
                pb, lambda c: tmB[:, c * 128:(c + 1) * 128],
                lambda c0, w: xqf.ap()[0:128, c0:c0 + w,
                                       t * 128:(t + 1) * 128],
                NDC, ps8[:, 2, :])
            if mode == 'l0':
                tmB, tmA = tmA, tmB
        if upto == "ln":
            dv.inc(nc.vector.tensor_copy(out=rres[:, 0, :], in_=tmA[:, :]))
            dv.chain()
            dv.inc(nc.vector.tensor_copy(out=rres[:, 1, :], in_=tmB[:, :]))
            dv.chain()
        gu_on = upto in ("gu", "hq", "full")
        if gu_on:
         with pb.loop(11, key="f1_gu") as p2:
            for sub in range(2):
                par = sub
                qw = qs if par == 0 else qsb
                pe.wait(qw)
                pe.wait(dv)
                pe.wait(ac)
                for t in range(2):
                    for c in range(NDC):
                        mm = nc.tensor.matmul(
                            ps8[:, t, :], xqf[:, c, t * 128:(t + 1) * 128],
                            wbuf[:, par, c, :],
                            start=(c == 0), stop=(c == NDC - 1))
                        if c == NDC - 1:
                            pe.inc(mm)
                qw.wait(pe)
                qw.wait(qw)
                qw.inc(qw.e.dma_start(out=wbuf[:, par, :, :],
                                      in_=w1T_ap[:, :, DS(wctr[sub].use(), 512)]))
                wctr[sub].adv()
                for t in range(2):
                    sc = ccol[:, 5 + t * 4:6 + t * 4]
                    ac.wait(pe)
                    ac.wait(dv)
                    if mode == 'l1':
                        ac.inc(nc.scalar.activation(
                            out=e256[:, t, :], in_=ps8[:, t, 0:256],
                            func=AF.Silu, scale=sc))
                    else:
                        ac.inc(nc.scalar.activation(
                            out=e256[:, t, :], in_=ps8[:, t, 0:256],
                            func=AF.Silu))
                    ac.chain()
                    dv.wait(pe)
                    dv.wait(ac)
                    if mode == 'l1':
                        dv.inc(nc.vector.tensor_scalar(
                            out=u256[:, t, :], in0=ps8[:, t, 256:512],
                            scalar1=sc, scalar2=None, op0=ALU.mult))
                    else:
                        dv.inc(nc.vector.tensor_copy(
                            out=u256[:, t, :], in_=ps8[:, t, 256:512]))
                    dv.chain()
                    dv.wait(ac)
                    dv.inc(nc.vector.tensor_tensor(
                        out=hbig.ap()[0:128, t, DS(p2 * 512 + sub * 256, 256)],
                        in0=e256[:, t, :], in1=u256[:, t, :], op=ALU.mult))
                    dv.chain()
        if upto == "gu":
            dv.inc(nc.vector.tensor_copy(out=rres[:, 0, :],
                                         in_=hbig[:, 0, 0:D]))
            dv.chain()
            dv.inc(nc.vector.tensor_copy(out=rres[:, 1, :],
                                         in_=hbig[:, 1, 0:D]))
            dv.chain()
        if upto in ("hq", "full"):
         for t in range(2):
            if mode == 'l1':
                emit_actquant(pb, hbig[:, t, :], hbig[:, t, :], ccol, 8 + 4 * t)
            transpose_f32_to_bf16(
                pb, lambda c: hbig[:, t, c * 128:(c + 1) * 128],
                lambda c0, w: hqf.ap()[0:128, c0:c0 + w,
                                       t * 128:(t + 1) * 128],
                NIC, ps8[:, 2, :])
         if upto == "hq":
            dv.inc(nc.vector.tensor_copy(out=rres[:, 0, :],
                                         in_=hbig[:, 0, 0:D]))
            dv.chain()
            dv.inc(nc.vector.tensor_copy(out=rres[:, 1, :],
                                         in_=hbig[:, 1, 0:D]))
            dv.chain()
        if upto == "full":
         for icl in range(NIC):
            par = icl % 2
            qw = qg if par == 0 else qgb
            pe.wait(qw)
            if icl == 0:
                pe.wait(dv)
            for t in range(2):
                for og in range(4):
                    mm = nc.tensor.matmul(
                        ps8[:, t * 4 + og, :],
                        hqf[:, icl, t * 128:(t + 1) * 128],
                        w2buf[:, par, og * 512:(og + 1) * 512],
                        start=(icl == 0), stop=(icl == NIC - 1))
                    if t == 1 and og == 3:
                        pe.inc(mm)
            nxt = (icl + 2) % NIC
            qw.wait(pe)
            qw.wait(qw)
            qw.inc(qw.e.dma_start(
                out=w2buf[:, par, :],
                in_=w2T_ap[:, nxt, :]))
        dv.wait(pe)
        dv.wait(qgb)
        if upto == "full" and mode == 'l1':
         for t in range(2):
            sch = ccol[:, 9 + 4 * t:10 + 4 * t]
            for og in range(4):
                dv.inc(nc.vector.scalar_tensor_tensor(
                    out=rres[:, t, og * 512:(og + 1) * 512],
                    in0=ps8[:, t * 4 + og, :], scalar=sch,
                    in1=rgrp[:, t, og * 512:(og + 1) * 512],
                    op0=ALU.mult, op1=ALU.add))
                dv.chain()
        if upto == "full" and mode == 'l0':
         for t in range(2):
            for og in range(4):
                dv.inc(nc.vector.tensor_copy(
                    out=rres[:, t, og * 512:(og + 1) * 512],
                    in0=ps8[:, t * 4 + og, :]) if False else
                    nc.vector.tensor_copy(
                        out=rres[:, t, og * 512:(og + 1) * 512],
                        in_=ps8[:, t * 4 + og, :]))
                dv.chain()
            rq = ccol[:, 9 + 4 * t:10 + 4 * t]
            ac.wait(dv)
            ac.inc(nc.scalar.activation(out=tmA[:, :], in_=rres[:, t, :],
                                        func=AF.Square))
            ac.chain()
            dv.wait(ac)
            dv.inc(nc.vector.reduce_sum(out=rq, in_=tmA[:, :], axis=AX))
            dv.chain()
            dv.inc(nc.vector.tensor_scalar(out=rq, in0=rq, scalar1=1.0 / D,
                                           scalar2=1e-6, op0=ALU.mult,
                                           op1=ALU.add))
            dv.chain()
            ac.wait(dv)
            ac.inc(nc.scalar.activation(out=rq, in_=rq, func=AF.Sqrt))
            ac.chain()
            dv.wait(ac)
            dv.inc(nc.vector.reciprocal(out=rq, in_=rq))
            dv.chain()
            dv.inc(nc.vector.scalar_tensor_tensor(
                out=rres[:, t, :], in0=rres[:, t, :], scalar=rq,
                in1=rgrp[:, t, :], op0=ALU.mult, op1=ALU.add))
            dv.chain()
        for t in range(2):
            qg.wait(dv)
            qg.wait(qg)
            qg.inc(qg.e.dma_start(
                out=rgrp[:, t, :],
                in_=bass.AP(Rin_ap.tensor, rctr[t].use(),
                            [[D, 128], [1, D]])))
            rctr[t].adv()
        qgb.wait(dv)
        qgb.inc(qgb.e.dma_start(
            out=bass.AP(Rout_ap.tensor, octr.use(),
                        [[D, 128], [128 * D, 2], [1, D]]),
            in_=rres[:, :, :]))
        octr.adv()
    pb.bar()




def build_l1ffn(registry=None, parts=None, upto='full'):
    pb = PB(registry)
    nc = pb.nc
    Rin = nc.dram_tensor("Rin", [NT, D], F32, kind="ExternalInput").ap()
    w1 = nc.dram_tensor("w_gate_up", [2 * DI, D], F32, kind="ExternalInput").ap()
    w2 = nc.dram_tensor("w_down", [D, DI], F32, kind="ExternalInput").ap()
    Rout = nc.dram_tensor("Rout", [NT, D], F32, kind="ExternalOutput").ap()
    w1T = nc.dram_tensor("w1T", [128, NDC, 2 * DI], BF16, kind="Internal").ap()
    w2T = nc.dram_tensor("w2T", [128, NIC, D], BF16, kind="Internal").ap()

    setup_common(pb)
    if parts is None or "w1" in parts:
        emit_prep_rowquant(pb, w1, w1T, 2 * DI, D, gu_pair_groups(), key="p_w1")
    if parts is None or "w2" in parts:
        emit_prep_rowquant(pb, w2, w2T, D, DI, plain_groups(D), key="p_w2")
    if parts is None or "main" in parts:
        emit_l1ffn_main(pb, Rin, Rout, w1T, w2T, upto=upto)
    else:
        s = pb.qg
        s.inc(pb.nc.gpsimd.dma_start(out=Rout[0:128, :], in_=Rin[0:128, :]))
        pb.bar()
    pb.es.close()
    return pb


def build_twopass(builder, *args, **kwargs):
    pb1 = builder(*args, registry=None, **kwargs)
    reg = pb1.new_registry
    pb2 = builder(*args, registry=reg, **kwargs)
    return pb2


def devrun_l1ffn(parts, seed=7, upto='full'):
    import time
    np.random.seed(seed)
    x = np.random.randn(NT, D).astype(np.float32)
    w1 = (np.random.randn(2 * DI, D) * D ** -0.5).astype(np.float32)
    w2 = (np.random.randn(D, DI) * DI ** -0.5).astype(np.float32)
    pb = build_twopass(build_l1ffn, parts=parts, upto=upto)
    t0 = time.time()
    try:
        res = run_bass_kernel_spmd(pb.nc, [dict(Rin=x, w_gate_up=w1,
                                                w_down=w2)], core_ids=[0])
        print(f"parts={parts} upto={upto}: OK after {time.time()-t0:.0f}s")
        return res
    except Exception as e:
        print(f"parts={parts} upto={upto}: FAILED after {time.time()-t0:.0f}s "
              f"{type(e).__name__}")
        return None




def emit_prep_fp4(pb, wp_ap, sc_ap, wT_ap, O, K, groups, key="pf4"):
    nc = pb.nc
    dv, ac, pe = pb.dv, pb.ac, pb.pe
    qs, qsb, qg = pb.qs, pb.qsb, pb.qg
    KC = K // 128
    KB = K // 2
    w3 = wp_ap.rearrange("(n p) k -> n p k", p=128)
    C = MAGIC
    LN2L = math.log(2.0)

    with contextlib.ExitStack() as es2:
        pbuf = es2.enter_context(nc.sbuf_tensor(key + "_pb", [128, 2, KB], U8))
        sbuf = es2.enter_context(nc.sbuf_tensor(key + "_sc", [KC, 2, 128], F16))
        scf = es2.enter_context(nc.sbuf_tensor(key + "_sf", [KC, 128], F32))
        scT = es2.enter_context(nc.sbuf_tensor(key + "_st", [128, KC], F32))
        bts = es2.enter_context(nc.sbuf_tensor(key + "_bt", [128, KB], F32))
        hi = es2.enter_context(nc.sbuf_tensor(key + "_hi", [128, KB], F32))
        lo = es2.enter_context(nc.sbuf_tensor(key + "_lo", [128, KB], F32))
        t0 = es2.enter_context(nc.sbuf_tensor(key + "_t0", [128, KB], F32))
        t1 = es2.enter_context(nc.sbuf_tensor(key + "_t1", [128, KB], F32))
        t2 = es2.enter_context(nc.sbuf_tensor(key + "_t2", [128, KB], F32))
        dq = es2.enter_context(nc.sbuf_tensor(key + "_dq", [128, K], F32))
        tq = es2.enter_context(nc.sbuf_tensor(key + "_tq", [128, KC, 128], BF16))
        bank = pb.ps8[:, 0, :]

        gsz = len(groups[0])
        ngrp = len(groups)
        base = [groups[0][j] for j in range(gsz)]
        step = ([(groups[1][j][0] - groups[0][j][0],
                  groups[1][j][1] - groups[0][j][1]) for j in range(gsz)]
                if ngrp > 1 else [(0, 0)] * gsz)

        def decode_plane(nib_ap, out_ap):
            ac.wait(dv)
            ac.inc(nc.scalar.activation(out=t0[:, :], in_=nib_ap, func=AF.Sign,
                                        bias=-7.5))
            ac.chain()
            dv.wait(ac)
            dv.inc(nc.vector.scalar_tensor_tensor(
                out=t1[:, :], in0=t0[:, :], scalar=4.0, in1=nib_ap,
                op0=ALU.mult, op1=ALU.subtract))
            dv.chain()
            dv.inc(nc.vector.tensor_scalar(out=t1[:, :], in0=t1[:, :],
                                           scalar1=-1.0, scalar2=-4.0,
                                           op0=ALU.mult, op1=ALU.add))
            dv.chain()
            dv.inc(nc.vector.tensor_scalar(out=t2[:, :], in0=t1[:, :],
                                           scalar1=-0.5, scalar2=0.5,
                                           op0=ALU.add, op1=ALU.mult))
            dv.chain()
            dv.inc(nc.vector.tensor_scalar(out=t2[:, :], in0=t2[:, :],
                                           scalar1=C, scalar2=C,
                                           op0=ALU.add, op1=ALU.subtract))
            dv.chain()
            dv.inc(nc.vector.scalar_tensor_tensor(
                out=t1[:, :], in0=t2[:, :], scalar=-2.0, in1=t1[:, :],
                op0=ALU.mult, op1=ALU.add))
            dv.chain()
            ac.wait(dv)
            ac.inc(nc.scalar.activation(out=nib_ap, in_=t2[:, :], func=AF.Exp,
                                        scale=LN2L, bias=-LN2L))
            ac.chain()
            dv.wait(ac)
            dv.inc(nc.vector.tensor_scalar(out=t2[:, :], in0=t2[:, :],
                                           scalar1=1.0, scalar2=-1.0,
                                           op0=ALU.min, op1=ALU.mult))
            dv.chain()
            dv.inc(nc.vector.tensor_tensor(out=t2[:, :], in0=t2[:, :],
                                           in1=nib_ap, op=ALU.mult))
            dv.chain()
            dv.inc(nc.vector.tensor_scalar(out=nib_ap, in0=nib_ap,
                                           scalar1=1.0, scalar2=None,
                                           op0=ALU.max))
            dv.chain()
            dv.inc(nc.vector.tensor_scalar(out=t1[:, :], in0=t1[:, :],
                                           scalar1=-0.5, scalar2=None,
                                           op0=ALU.mult))
            dv.chain()
            dv.inc(nc.vector.tensor_tensor(out=t1[:, :], in0=t1[:, :],
                                           in1=nib_ap, op=ALU.mult))
            dv.chain()
            dv.inc(nc.vector.tensor_tensor(out=t1[:, :], in0=t1[:, :],
                                           in1=t2[:, :], op=ALU.add))
            dv.chain()
            dv.inc(nc.vector.tensor_tensor(out=out_ap, in0=t1[:, :],
                                           in1=t0[:, :], op=ALU.mult))
            dv.chain()

        for j in range(min(gsz, 2)):
            q = qs if j % 2 == 0 else qsb
            q.inc(q.e.dma_start(out=pbuf[:, j % 2, :],
                                in_=w3[base[j][0], :, :]))
            q.inc(q.e.dma_start(
                out=sbuf[:, j % 2, :],
                in_=sc_ap[:, base[j][0] * 128:(base[j][0] + 1) * 128]))

        TSZ = 128 * KB
        cmax = (O // 128 - 1) * TSZ
        SMX = O - 128
        ctr = []
        sctr = []
        for j in range(gsz):
            nxt = j + 2
            q = qs if j % 2 == 0 else qsb
            sl = nxt - gsz if nxt >= gsz else None
            if nxt < gsz:
                t0j, stj = base[nxt][0], step[nxt][0]
            else:
                t0j, stj = base[sl][0] + step[sl][0], step[sl][0]
            ctr.append(pb.counter(q, f"ctr{j}", t0j * TSZ, stj * TSZ,
                                  cap=cmax))
            sctr.append(pb.counter(q, f"str{j}", t0j * 128, stj * 128,
                                   cap=SMX))
        dctr = [pb.counter(pb.qg, f"dtr{j}", base[j][1], step[j][1],
                           cap=base[j][1] + step[j][1] * max(ngrp - 1, 1))
                for j in range(gsz)]

        with pb.loop(ngrp, key=key) as g:
            for j in range(gsz):
                par = j % 2
                q = qs if par == 0 else qsb
                dv.wait(q)
                dv.wait(qg)
                dv.wait(pe)
                dv.inc(nc.vector.tensor_copy(out=scf[:, :],
                                             in_=sbuf[:, par, :]))
                dv.chain()
                pe.wait(dv)
                pe.inc(nc.tensor.transpose(bank[:, 0:KC], scf[:, :],
                                           pb.ident32[0:KC, 0:KC]))
                dv.wait(pe)
                dv.inc(nc.vector.tensor_copy(out=scT[:, :], in_=bank[:, 0:KC]))
                dv.chain()
                dv.inc(nc.vector.tensor_copy(out=bts[:, :],
                                             in_=pbuf[:, par, :]))
                dv.chain()
                dv.inc(nc.vector.tensor_scalar(
                    out=hi[:, :], in0=bts[:, :], scalar1=-7.5,
                    scalar2=1.0 / 16.0, op0=ALU.add, op1=ALU.mult))
                dv.chain()
                dv.inc(nc.vector.tensor_scalar(out=hi[:, :], in0=hi[:, :],
                                               scalar1=C, scalar2=C,
                                               op0=ALU.add, op1=ALU.subtract))
                dv.chain()
                dv.inc(nc.vector.scalar_tensor_tensor(
                    out=lo[:, :], in0=hi[:, :], scalar=-16.0, in1=bts[:, :],
                    op0=ALU.mult, op1=ALU.add))
                dv.chain()
                q.wait(dv)
                q.wait(q)
                q.inc(q.e.dma_start(
                    out=pbuf[:, par, :],
                    in_=bass.AP(wp_ap.tensor, ctr[j].use(),
                                [[KB, 128], [1, KB]])))
                q.inc(q.e.dma_start(
                    out=sbuf[:, par, :],
                    in_=bass.AP(sc_ap.tensor, sctr[j].use(),
                                [[O, KC], [1, 128]])))
                ctr[j].adv()
                sctr[j].adv()
                dq2 = dq.ap().rearrange("p (a b) -> p a b", b=2)
                decode_plane(lo[:, :], dq2[:, :, 0:1].rearrange("p a b -> p (a b)"))
                decode_plane(hi[:, :], dq2[:, :, 1:2].rearrange("p a b -> p (a b)"))
                for c in range(KC):
                    dv.inc(nc.vector.tensor_scalar(
                        out=dq[:, c * 128:(c + 1) * 128],
                        in0=dq[:, c * 128:(c + 1) * 128],
                        scalar1=scT[:, c:c + 1], scalar2=None, op0=ALU.mult))
                    dv.chain()
                transpose_f32_to_bf16(
                    pb, lambda c: dq[:, c * 128:(c + 1) * 128],
                    lambda c0, w: tq[:, c0:c0 + w, :],
                    KC, bank)
                qg.wait(dv)
                qg.inc(qg.e.dma_start(
                    out=wT_ap[:, :, DS(dctr[j].use(), 128)],
                    in_=tq[:, :, :]))
                dctr[j].adv()
        pb.bar()


def build_l0ffn(registry=None, parts=None, upto='full'):
    pb = PB(registry)
    nc = pb.nc
    Rin = nc.dram_tensor("Rin", [NT, D], F32, kind="ExternalInput").ap()
    gup = nc.dram_tensor("gu_packed_u8", [2 * DI, D // 2], U8,
                         kind="ExternalInput").ap()
    gus = nc.dram_tensor("gu_scales", [D // 128, 2 * DI], F16,
                         kind="ExternalInput").ap()
    dnp = nc.dram_tensor("dn_packed_u8", [D, DI // 2], U8,
                         kind="ExternalInput").ap()
    dns = nc.dram_tensor("dn_scales", [DI // 128, D], F16,
                         kind="ExternalInput").ap()
    Rout = nc.dram_tensor("Rout", [NT, D], F32, kind="ExternalOutput").ap()
    w1T = nc.dram_tensor("w1T", [128, NDC, 2 * DI], BF16, kind="Internal").ap()
    w2T = nc.dram_tensor("w2T", [128, NIC, D], BF16, kind="Internal").ap()

    setup_common(pb)
    if parts is None or "w1" in parts:
        emit_prep_fp4(pb, gup, gus, w1T, 2 * DI, D, gu_pair_groups(),
                      key="pf4a")
    if parts is None or "w2" in parts:
        emit_prep_fp4(pb, dnp, dns, w2T, D, DI, plain_groups(D), key="pf4b")
    if parts is None or "main" in parts:
        emit_l1ffn_main(pb, Rin, Rout, w1T, w2T, upto=upto, mode='l0')
    else:
        s = pb.qg
        s.inc(pb.nc.gpsimd.dma_start(out=Rout[0:128, :], in_=Rin[0:128, :]))
        pb.bar()
    pb.es.close()
    return pb




def emit_prep_ternary(pb, w_ap, wT_ap, gwb_col, onescol, key="pt"):
    nc = pb.nc
    dv, pe = pb.dv, pb.pe
    qs, qsb, qg = pb.qs, pb.qsb, pb.qg
    O = K = D
    KC = NDC
    w3 = w_ap.rearrange("(n p) k -> n p k", p=128)

    with contextlib.ExitStack() as es2:
        pbuf = es2.enter_context(nc.sbuf_tensor(key + "_pb", [128, 2, K], F32))
        wq16 = es2.enter_context(nc.sbuf_tensor(key + "_wq", [128, K], F32))
        tq = es2.enter_context(nc.sbuf_tensor(key + "_tq", [128, KC, 128], BF16))
        acc = es2.enter_context(nc.sbuf_tensor(key + "_ac", [128, 2], F32))
        sca = es2.enter_context(nc.sbuf_tensor(key + "_sc", [1, 2], F32))
        rqc = es2.enter_context(nc.sbuf_tensor(key + "_rq", [128, 1], F32))
        bank = pb.ps8[:, 0, :]
        gp = pb.gp

        gp.inc(nc.gpsimd.memset(acc[:, :], 0.0))
        qs.inc(qs.e.dma_start(out=pbuf[:, 0, :], in_=w3[0, :, :]))
        qsb.inc(qsb.e.dma_start(out=pbuf[:, 1, :], in_=w3[1, :, :]))
        TSZ = 128 * K
        ctrA = [pb.counter(qs, "ctr0", 2 * TSZ, 2 * TSZ,
                           cap=(O // 128 - 1) * TSZ),
                pb.counter(qsb, "ctr1", 3 * TSZ, 2 * TSZ,
                           cap=(O // 128 - 1) * TSZ)]
        dv.wait(gp)
        with pb.loop(O // 256, key=key + "A") as g:
            for par in range(2):
                q = qs if par == 0 else qsb
                dv.wait(q)
                dv.inc(nc.vector.tensor_reduce(
                    out=acc[:, 1:2], in_=pbuf[:, par, :], axis=AX,
                    op=ALU.add, apply_absolute_value=True))
                dv.chain()
                q.wait(dv)
                q.wait(q)
                q.inc(q.e.dma_start(
                    out=pbuf[:, par, :],
                    in_=bass.AP(w_ap.tensor, ctrA[par].use(),
                                [[K, 128], [1, K]])))
                ctrA[par].adv()
                dv.inc(nc.vector.tensor_tensor(out=acc[:, 0:1], in0=acc[:, 0:1],
                                               in1=acc[:, 1:2], op=ALU.add))
                dv.chain()
        pe.wait(dv)
        pe.inc(nc.tensor.matmul(bank[0:1, 0:1], acc[:, 0:1], onescol[:, 0:1],
                                start=True, stop=True))
        dv.wait(pe)
        dv.inc(nc.vector.tensor_scalar(out=sca[:, 0:1], in0=bank[0:1, 0:1],
                                       scalar1=1.0 / (2048.0 * 2048.0),
                                       scalar2=1e-5, op0=ALU.mult, op1=ALU.max))
        dv.chain()
        dv.inc(nc.vector.reciprocal(out=sca[:, 1:2], in_=sca[:, 0:1]))
        dv.chain()
        pe.wait(dv)
        pe.inc(nc.tensor.matmul(bank[:, 0:2], pb.onesrow[0:1, :],
                                sca[0:1, 0:2], start=True, stop=True))
        dv.wait(pe)
        dv.inc(nc.vector.tensor_copy(out=gwb_col, in_=bank[:, 0:1]))
        dv.chain()
        dv.inc(nc.vector.tensor_copy(out=rqc[:, :], in_=bank[:, 1:2]))
        dv.chain()

        qs.wait(qs)
        qsb.wait(qsb)
        qs.inc(qs.e.dma_start(out=pbuf[:, 0, :], in_=w3[0, :, :]))
        qsb.inc(qsb.e.dma_start(out=pbuf[:, 1, :], in_=w3[1, :, :]))
        ctrB = [pb.counter(qs, "ctr0", 2 * TSZ, 2 * TSZ,
                           cap=(O // 128 - 1) * TSZ),
                pb.counter(qsb, "ctr1", 3 * TSZ, 2 * TSZ,
                           cap=(O // 128 - 1) * TSZ)]
        dctr2 = [pb.counter(qg, "dta0", 0, 256, cap=(O // 128 - 2) * 128),
                 pb.counter(qg, "dta1", 128, 256, cap=(O // 128 - 1) * 128)]
        with pb.loop(O // 256, key=key + "B") as g:
            for par in range(2):
                q = qs if par == 0 else qsb
                dv.wait(q)
                dv.wait(qg)
                dv.wait(pe)
                dv.inc(nc.vector.tensor_scalar(
                    out=wq16[:, :], in0=pbuf[:, par, :], scalar1=rqc[:, 0:1],
                    scalar2=MAGIC, op0=ALU.mult, op1=ALU.add))
                dv.chain()
                dv.inc(nc.vector.tensor_scalar(
                    out=wq16[:, :], in0=wq16[:, :], scalar1=MAGIC,
                    scalar2=-1.0, op0=ALU.subtract, op1=ALU.max))
                dv.chain()
                dv.inc(nc.vector.tensor_scalar(
                    out=wq16[:, :], in0=wq16[:, :], scalar1=1.0, scalar2=None,
                    op0=ALU.min))
                dv.chain()
                q.wait(dv)
                q.wait(q)
                q.inc(q.e.dma_start(
                    out=pbuf[:, par, :],
                    in_=bass.AP(w_ap.tensor, ctrB[par].use(),
                                [[K, 128], [1, K]])))
                ctrB[par].adv()
                transpose_f32_to_bf16(
                    pb, lambda c: wq16[:, c * 128:(c + 1) * 128],
                    lambda c0, w: tq[:, c0:c0 + w, :],
                    KC, bank)
                qg.wait(dv)
                qg.inc(qg.e.dma_start(
                    out=wT_ap[:, :, DS(dctr2[par].use(), 128)],
                    in_=tq[:, :, :]))
                dctr2[par].adv()
        pb.bar()




def emit_l1attn(pb, Rin_ap, Rout_ap, wTs, gwb, nwbc, gnw_col, hmid_ap,
                hend_ap, prefix_only=False):
    nc = pb.nc
    dv, ac, pe, gp = pb.dv, pb.ac, pb.pe, pb.gp
    qs, qsb, qg, qgb = pb.qs, pb.qsb, pb.qg, pb.qgb
    NG = 16
    GT = 256
    ps8 = pb.ps8
    IB = {"i": 0, "f": 1, "g": 2, "o": 3}

    rt = pb.sb("a1_rt", [128, D], F32)
    z = pb.sb("a1_z", [128, 2, D], F32)
    tmB = pb.sb("a1_tmB", [128, D], F32)
    xqf = pb.sb("a1_xqf", [128, NDC, GT], BF16)
    wbuf = pb.sb("a1_wbuf", [128, 2, NDC, 512], BF16)
    fFM = pb.sb("a1_f", [128, NDC, GT], F32)
    iaFM = pb.sb("a1_ia", [128, NDC, GT], F32)
    hFM = pb.sb("a1_h", [128, NDC, GT], F32)
    gFM = pb.sb("a1_g", [128, NDC, GT], F32)
    carry = pb.sb("a1_carry", [128, NDC], F32)
    cols = pb.sb("a1_cols", [128, 16], F32)
    rows = pb.sb("a1_rows", [1, 8 * GT], F32)
    osbc = pb.sb("a1_osbc", [128, 3, GT], F32)
    e256 = pb.sb("a1_e256", [128, GT], F32)

    Rin3 = Rin_ap.rearrange("(a p) d -> p a d", p=128)
    Rout3 = Rout_ap.rearrange("(a p) d -> p a d", p=128)

    qg.inc(qg.e.dma_start(out=carry[:, :], in_=hmid_ap[:, :]))
    RT = 128 * D
    rctr = [pb.counter(qg, "ra0", 0, 2 * RT, cap=(2 * NG - 2) * RT),
            pb.counter(qg, "ra1", RT, 2 * RT, cap=(2 * NG - 1) * RT)]
    rctr2 = [pb.counter(qg, "rb0", 0, 2 * RT, cap=(2 * NG - 2) * RT),
             pb.counter(qg, "rb1", RT, 2 * RT, cap=(2 * NG - 1) * RT)]
    octr = [pb.counter(qgb, "oa0", 0, 2 * RT, cap=(2 * NG - 2) * RT),
            pb.counter(qgb, "oa1", RT, 2 * RT, cap=(2 * NG - 1) * RT)]
    worder = ["f", "i"] if prefix_only else ["f", "i", "g", "o"]
    NSP = len(worder) * 4
    wctr = None

    def span_src(pos):
        b = worder[pos // 4]
        sp = pos % 4
        return wTs[b][:, :, sp * 512:(sp + 1) * 512]

    qs.inc(qs.e.dma_start(out=wbuf[:, 0, :, :], in_=span_src(0)))
    qsb.inc(qsb.e.dma_start(out=wbuf[:, 1, :, :], in_=span_src(1)))

    with pb.loop(NG, key="a1_main" + ("P" if prefix_only else "")) as g:
        dv.wait(pe)
        dv.wait(qgb)
        for t in range(2):
            qg.wait(qg)
            qg.wait(dv)
            qg.inc(qg.e.dma_start(
                out=rt[:, :],
                in_=bass.AP(Rin_ap.tensor, rctr[t].use(), [[D, 128], [1, D]])))
            rctr[t].adv()
            dv.wait(qg)
            ln_tile(pb, rt[:, :], z[:, t, :], tmB[:, :], cols, 0)
        for b in (["f", "i"] if prefix_only else ["f", "i", "g"]):
            for t in range(2):
                dv.inc(nc.vector.tensor_tensor(
                    out=tmB[:, :], in0=z[:, t, :], in1=nwbc[b][:, :],
                    op=ALU.mult))
                dv.chain()
                am = cols[:, 4:5]
                c1 = cols[:, 5:6]
                rc = cols[:, 6:7]
                sc = cols[:, 7:8]
                dv.inc(nc.vector.reduce_max(out=am, in_=tmB[:, :], axis=AX,
                                            apply_absolute_value=True))
                dv.chain()
                dv.inc(nc.vector.tensor_scalar(out=c1, in0=am, scalar1=1e-5,
                                               scalar2=None, op0=ALU.max))
                dv.chain()
                dv.inc(nc.vector.reciprocal(out=rc, in_=c1))
                dv.chain()
                dv.inc(nc.vector.tensor_scalar(out=sc, in0=rc, scalar1=127.0,
                                               scalar2=None, op0=ALU.mult))
                dv.chain()
                dv.inc(nc.vector.scalar_tensor_tensor(
                    out=cols[:, 8 + t:9 + t], in0=c1, scalar=1.0 / 127.0,
                    in1=gwb[:, IB[b]:IB[b] + 1], op0=ALU.mult, op1=ALU.mult))
                dv.chain()
                dv.inc(nc.vector.tensor_scalar(out=tmB[:, :], in0=tmB[:, :],
                                               scalar1=sc, scalar2=MAGIC,
                                               op0=ALU.mult, op1=ALU.add))
                dv.chain()
                dv.inc(nc.vector.tensor_scalar(out=tmB[:, :], in0=tmB[:, :],
                                               scalar1=MAGIC, scalar2=None,
                                               op0=ALU.subtract))
                dv.chain()
                transpose_f32_to_bf16(
                    pb, lambda c: tmB[:, c * 128:(c + 1) * 128],
                    lambda c0, w: xqf.ap()[0:128, c0:c0 + w,
                                           t * 128:(t + 1) * 128],
                    NDC, ps8[:, 4, :])
                pe.wait(dv)
                pe.inc(nc.tensor.transpose(ps8[0:1, 5, 0:128],
                                           cols[:, 8 + t:9 + t],
                                           pb.ident32[:]))
                dv.wait(pe)
                dv.inc(nc.vector.tensor_copy(
                    out=rows[:, t * 128:(t + 1) * 128],
                    in_=ps8[0:1, 5, 0:128]))
                dv.chain()
            pe.wait(dv)
            pe.inc(nc.tensor.matmul(ps8[:, 5, 0:GT], pb.onesrow[0:1, :],
                                    rows[0:1, 0:GT], start=True, stop=True))
            dv.wait(pe)
            dv.inc(nc.vector.tensor_copy(out=osbc[:, IB[b], :],
                                         in_=ps8[:, 5, 0:GT]))
            dv.chain()
            wpos0 = worder.index(b) * 4
            for sp in range(4):
                pos = wpos0 + sp
                par = pos % 2
                qw = qs if par == 0 else qsb
                pe.wait(qw)
                pe.wait(dv)
                pe.wait(ac)
                for dc in range(NDC):
                    for oc in range(4):
                        mm = nc.tensor.matmul(
                            ps8[:, oc, 0:GT],
                            wbuf[:, par, dc, oc * 128:(oc + 1) * 128],
                            xqf[:, dc, :],
                            start=(dc == 0), stop=(dc == NDC - 1))
                        if dc == NDC - 1 and oc == 3:
                            pe.inc(mm)
                qw.wait(pe)
                qw.wait(qw)
                nxt = (pos + 2) % NSP
                qw.inc(qw.e.dma_start(out=wbuf[:, par, :, :],
                                      in_=span_src(nxt)))
                oco = sp * 4
                for oc in range(4):
                    src = ps8[:, oc, 0:GT]
                    dst_c = oco + oc
                    dv.wait(pe)
                    dv.wait(ac)
                    if b == "f":
                        dv.inc(nc.vector.tensor_tensor(
                            out=e256[:, :], in0=src, in1=osbc[:, 1, :],
                            op=ALU.mult))
                        dv.chain()
                        ac.wait(dv)
                        ac.inc(nc.scalar.activation(
                            out=fFM[:, dst_c, :], in_=e256[:, :],
                            func=AF.Sigmoid))
                        ac.chain()
                    elif b == "i":
                        dv.inc(nc.vector.tensor_tensor(
                            out=e256[:, :], in0=src, in1=osbc[:, 0, :],
                            op=ALU.mult))
                        dv.chain()
                        ac.wait(dv)
                        ac.inc(nc.scalar.activation(
                            out=e256[:, :], in_=e256[:, :], func=AF.Silu))
                        ac.chain()
                        dv.wait(ac)
                        dv.inc(nc.vector.tensor_scalar(
                            out=iaFM[:, dst_c, :], in0=fFM[:, dst_c, :],
                            scalar1=-1.0, scalar2=1.0, op0=ALU.mult,
                            op1=ALU.add))
                        dv.chain()
                        dv.inc(nc.vector.tensor_tensor(
                            out=iaFM[:, dst_c, :], in0=iaFM[:, dst_c, :],
                            in1=e256[:, :], op=ALU.mult))
                        dv.chain()
                    else:
                        dv.inc(nc.vector.tensor_tensor(
                            out=gFM[:, dst_c, :], in0=src, in1=osbc[:, 2, :],
                            op=ALU.mult))
                        dv.chain()
        for dc in range(NDC):
            dv.inc(nc.vector.tensor_tensor_scan(
                out=hFM[:, dc, :], data0=fFM[:, dc, :], data1=iaFM[:, dc, :],
                initial=carry[:, dc:dc + 1], op0=ALU.mult, op1=ALU.add))
            dv.chain()
        dv.inc(nc.vector.tensor_copy(
            out=carry[:, :],
            in_=hFM.ap()[0:128, :, GT - 1:GT].rearrange("p a b -> p (a b)")))
        dv.chain()
        if not prefix_only:
            ac.wait(dv)
            ac.inc(nc.scalar.activation(
                out=fFM.ap()[0:128, :, :].rearrange("p a b -> p (a b)"),
                in_=gFM.ap()[0:128, :, :].rearrange("p a b -> p (a b)"),
                func=AF.Square))
            ac.chain()
            pe.wait(ac)
            for dc in range(NDC):
                mm = nc.tensor.matmul(ps8[0:1, 5, 0:GT], pb.onescol32[:, 0:1],
                                      fFM[:, dc, :],
                                      start=(dc == 0), stop=(dc == NDC - 1))
                if dc == NDC - 1:
                    pe.inc(mm)
            dv.wait(pe)
            r0 = rows[:, 0:GT]
            dv.inc(nc.vector.tensor_scalar(out=r0, in0=ps8[0:1, 5, 0:GT],
                                           scalar1=1.0 / D, scalar2=1e-5,
                                           op0=ALU.mult, op1=ALU.add))
            dv.chain()
            ac.wait(dv)
            ac.inc(nc.scalar.activation(out=r0, in_=r0, func=AF.Sqrt))
            ac.chain()
            dv.wait(ac)
            dv.inc(nc.vector.reciprocal(out=r0, in_=r0))
            dv.chain()
            pe.wait(dv)
            pe.inc(nc.tensor.matmul(ps8[:, 5, 0:GT], pb.onesrow[0:1, :],
                                    rows[0:1, 0:GT], start=True, stop=True))
            ac.wait(dv)
            ac.inc(nc.scalar.activation(
                out=iaFM.ap()[0:128, :, :].rearrange("p a b -> p (a b)"),
                in_=hFM.ap()[0:128, :, :].rearrange("p a b -> p (a b)"),
                func=AF.Silu))
            ac.chain()
            dv.wait(pe)
            dv.wait(ac)
            for dc in range(NDC):
                dv.inc(nc.vector.scalar_tensor_tensor(
                    out=gFM[:, dc, :], in0=gFM[:, dc, :],
                    scalar=gnw_col[:, dc:dc + 1], in1=ps8[:, 5, 0:GT],
                    op0=ALU.mult, op1=ALU.mult))
                dv.chain()
                dv.inc(nc.vector.tensor_tensor(
                    out=gFM[:, dc, :], in0=gFM[:, dc, :], in1=iaFM[:, dc, :],
                    op=ALU.mult))
                dv.chain()
            for t in range(2):
                pe.wait(dv)
                for c0 in range(0, NDC, 4):
                    pe.wait(dv)
                    for c in range(c0, c0 + 4):
                        tr = nc.tensor.transpose(
                            ps8[:, 4, (c - c0) * 128:(c - c0 + 1) * 128],
                            gFM[:, c, t * 128:(t + 1) * 128], pb.ident32[:])
                        if c == c0 + 3:
                            pe.inc(tr)
                    dv.wait(pe)
                    dv.inc(nc.vector.tensor_copy(
                        out=z[:, t, c0 * 128:(c0 + 4) * 128],
                        in_=ps8[:, 4, 0:512]))
                    dv.chain()
            for t in range(2):
                am = cols[:, 4:5]
                c1 = cols[:, 5:6]
                rc = cols[:, 6:7]
                sc = cols[:, 7:8]
                ac.wait(dv)
                ac.inc(nc.scalar.activation(out=tmB[:, :], in_=z[:, t, :],
                                            func=AF.Square))
                ac.chain()
                dv.wait(ac)
                dv.inc(nc.vector.reduce_sum(out=am, in_=tmB[:, :], axis=AX))
                dv.chain()
                dv.inc(nc.vector.tensor_scalar(out=am, in0=am, scalar1=1.0 / D,
                                               scalar2=1e-8, op0=ALU.mult,
                                               op1=ALU.add))
                dv.chain()
                ac.wait(dv)
                ac.inc(nc.scalar.activation(out=am, in_=am, func=AF.Sqrt))
                ac.chain()
                dv.wait(ac)
                dv.inc(nc.vector.reciprocal(out=am, in_=am))
                dv.chain()
                dv.inc(nc.vector.tensor_scalar(out=z.ap()[0:128, t, :],
                                               in0=z[:, t, :], scalar1=am,
                                               scalar2=None, op0=ALU.mult))
                dv.chain()
                dv.inc(nc.vector.tensor_tensor(out=tmB[:, :], in0=z[:, t, :],
                                               in1=nwbc["o"][:, :],
                                               op=ALU.mult))
                dv.chain()
                dv.inc(nc.vector.reduce_max(out=am, in_=tmB[:, :], axis=AX,
                                            apply_absolute_value=True))
                dv.chain()
                dv.inc(nc.vector.tensor_scalar(out=c1, in0=am, scalar1=1e-5,
                                               scalar2=None, op0=ALU.max))
                dv.chain()
                dv.inc(nc.vector.reciprocal(out=rc, in_=c1))
                dv.chain()
                dv.inc(nc.vector.tensor_scalar(out=sc, in0=rc, scalar1=127.0,
                                               scalar2=None, op0=ALU.mult))
                dv.chain()
                dv.inc(nc.vector.scalar_tensor_tensor(
                    out=cols[:, 10 + t:11 + t], in0=c1, scalar=1.0 / 127.0,
                    in1=gwb[:, 3:4], op0=ALU.mult, op1=ALU.mult))
                dv.chain()
                dv.inc(nc.vector.tensor_scalar(out=tmB[:, :], in0=tmB[:, :],
                                               scalar1=sc, scalar2=MAGIC,
                                               op0=ALU.mult, op1=ALU.add))
                dv.chain()
                dv.inc(nc.vector.tensor_scalar(out=tmB[:, :], in0=tmB[:, :],
                                               scalar1=MAGIC, scalar2=None,
                                               op0=ALU.subtract))
                dv.chain()
                transpose_f32_to_bf16(
                    pb, lambda c: tmB[:, c * 128:(c + 1) * 128],
                    lambda c0, w: xqf.ap()[0:128, c0:c0 + w,
                                           t * 128:(t + 1) * 128],
                    NDC, ps8[:, 4, :])
            for sp in range(4):
                pos = 12 + sp
                par = pos % 2
                qw = qs if par == 0 else qsb
                pe.wait(qw)
                pe.wait(dv)
                for t in range(2):
                    for dc in range(NDC):
                        mm = nc.tensor.matmul(
                            ps8[:, t * 4 + sp, :],
                            xqf[:, dc, t * 128:(t + 1) * 128],
                            wbuf[:, par, dc, :],
                            start=(dc == 0), stop=(dc == NDC - 1))
                        if dc == NDC - 1:
                            pe.inc(mm)
                qw.wait(pe)
                qw.wait(qw)
                nxt = (pos + 2) % NSP
                qw.inc(qw.e.dma_start(out=wbuf[:, par, :, :],
                                      in_=span_src(nxt)))
            for t in range(2):
                qg.wait(qg)
                qg.wait(dv)
                qg.inc(qg.e.dma_start(
                    out=rt[:, :],
                    in_=bass.AP(Rin_ap.tensor, rctr2[t].use(),
                                [[D, 128], [1, D]])))
                rctr2[t].adv()
                dv.wait(qg)
                dv.wait(pe)
                for sp in range(4):
                    dv.inc(nc.vector.scalar_tensor_tensor(
                        out=z.ap()[0:128, t, sp * 512:(sp + 1) * 512],
                        in0=ps8[:, t * 4 + sp, :],
                        scalar=cols[:, 10 + t:11 + t],
                        in1=rt[:, sp * 512:(sp + 1) * 512],
                        op0=ALU.mult, op1=ALU.add))
                    dv.chain()
                qgb.wait(dv)
                qgb.inc(qgb.e.dma_start(
                    out=bass.AP(Rout_ap.tensor, octr[t].use(),
                                [[D, 128], [1, D]]),
                    in_=z[:, t, :]))
                octr[t].adv()
    qg.wait(dv)
    qg.inc(qg.e.dma_start(out=hend_ap[:, :], in_=carry[:, :]))
    pb.bar()


def setup_l1attn_consts(pb, norm_aps):
    nc = pb.nc
    dv, pe, gp, qg = pb.dv, pb.pe, pb.gp, pb.qg
    nwbc = {}
    rowbuf = pb.sb("a1c_row", [1, D], F32)
    g16 = pb.sb("a1c_g16", [16, 128], F32)
    gnw_col = pb.sb("a1c_gnw", [128, NDC], F32)
    for b in ("i", "f", "g", "o"):
        nwbc[b] = pb.sb(f"a1c_nw{b}", [128, D], F32)
    for b in ("i", "f", "g", "o"):
        src = norm_aps[b]
        qg.wait(dv)
        qg.inc(qg.e.dma_start(out=rowbuf[:, :], in_=src.rearrange("(a d) -> a d", a=1)))
        dv.wait(qg)
        pe.wait(qg)
        for k in range(4):
            pe.wait(dv)
            pe.inc(nc.tensor.matmul(pb.ps8[:, 6, 0:512], pb.onesrow[0:1, :],
                                    rowbuf[0:1, k * 512:(k + 1) * 512],
                                    start=True, stop=True))
            dv.wait(pe)
            dv.inc(nc.vector.tensor_copy(out=nwbc[b][:, k * 512:(k + 1) * 512],
                                         in_=pb.ps8[:, 6, 0:512]))
            dv.chain()
    qg.wait(dv)
    qg.inc(qg.e.dma_start(out=g16[:, :],
                          in_=norm_aps["gn"].rearrange("(a d) -> a d", a=16)))
    dv.wait(qg)
    pe.wait(qg)
    pe.wait(dv)
    pe.inc(nc.tensor.transpose(pb.ps8[:, 6, 0:16], g16[:, :],
                               pb.ident32[0:16, 0:16]))
    dv.wait(pe)
    dv.inc(nc.vector.tensor_copy(out=gnw_col[:, :], in_=pb.ps8[:, 6, 0:16]))
    dv.chain()
    pb.bar()
    return nwbc, gnw_col


def build_l1attn(registry=None, prefix_only=False, parts=None):
    pb = PB(registry)
    nc = pb.nc
    pb.onesrow = pb.sb("g_ones_r", [1, 128], F32)
    pb.onescol32 = pb.sb("g_ones_c", [128, 1], F32)
    Rin = nc.dram_tensor("Rin", [NT, D], F32, kind="ExternalInput").ap()
    Wi = nc.dram_tensor("Wi", [D, D], F32, kind="ExternalInput").ap()
    Wf = nc.dram_tensor("Wf", [D, D], F32, kind="ExternalInput").ap()
    Wg = nc.dram_tensor("Wg", [D, D], F32, kind="ExternalInput").ap()
    Wo = nc.dram_tensor("Wo", [D, D], F32, kind="ExternalInput").ap()
    nmi = nc.dram_tensor("norm_i", [D], F32, kind="ExternalInput").ap()
    nmf = nc.dram_tensor("norm_f", [D], F32, kind="ExternalInput").ap()
    nmg = nc.dram_tensor("norm_g", [D], F32, kind="ExternalInput").ap()
    nmo = nc.dram_tensor("norm_o", [D], F32, kind="ExternalInput").ap()
    gnw = nc.dram_tensor("g_norm_w", [D], F32, kind="ExternalInput").ap()
    hmid = nc.dram_tensor("hmid", [128, NDC], F32, kind="ExternalInput").ap()
    Rout = nc.dram_tensor("Rout", [NT, D], F32, kind="ExternalOutput").ap()
    hend = nc.dram_tensor("hend", [128, NDC], F32, kind="ExternalOutput").ap()
    wTs = {b: nc.dram_tensor(f"wT{b}", [128, NDC, D], BF16,
                             kind="Internal").ap()
           for b in ("i", "f", "g", "o")}
    gwb = pb.sb("a1_gwb", [128, 4], F32)

    setup_common(pb)
    gp = pb.gp
    gp.inc(nc.gpsimd.memset(pb.onesrow[:, :], 1.0))
    gp.inc(nc.gpsimd.memset(pb.onescol32[:, :], 1.0))
    pb.bar()
    nwbc, gnw_col = setup_l1attn_consts(
        pb, dict(i=nmi, f=nmf, g=nmg, o=nmo, gn=gnw))
    wsrc = dict(i=Wi, f=Wf, g=Wg, o=Wo)
    blist = ["i", "f"] if prefix_only else ["i", "f", "g", "o"]
    for b in blist:
        if parts is None or b in parts:
            emit_prep_ternary(pb, wsrc[b], wTs[b], gwb[:, "ifgo".index(b):
                                                       "ifgo".index(b) + 1],
                              pb.onescol32, key=f"pt{b}")
    if parts is None or "main" in parts:
        emit_l1attn(pb, Rin, Rout, wTs, gwb, nwbc, gnw_col, hmid, hend,
                    prefix_only=prefix_only)
    else:
        pb.qg.inc(nc.gpsimd.dma_start(out=Rout[0:128, :], in_=Rin[0:128, :]))
        pb.qg.inc(nc.gpsimd.dma_start(out=hend[:, :], in_=hmid[:, :]))
        pb.bar()
    pb.es.close()
    return pb




def emit_l0attn(pb, Rin_ap, Rout_ap, slope_ap, wqkT_ap, woT_ap, kvmid_ap,
                kvend_ap, qkvu_ap, prefix_only=False):
    nc = pb.nc
    dv, ac, pe, gp = pb.dv, pb.ac, pb.pe, pb.gp
    qs, qsb, qg, qgb = pb.qs, pb.qsb, pb.qg, pb.qgb
    NH, HD, BLK = 16, 128, 256
    NB = NT // BLK
    ps8 = pb.ps8
    QW = 8192

    rrow = pb.sb("a0_rrow", [1, 16], F32)
    rbc = pb.sb("a0_rbc", [128, 16], F32)
    bdc = pb.sb("a0_bd", [128, 16], F32)
    biasc = pb.sb("a0_bias", [128, 4], F32)
    iotac = pb.sb("a0_iota", [128, 1], F32)
    deltp = pb.sb("a0_delt", [128, 2, BLK], F32)
    caus = pb.sb("a0_caus", [128, 2, BLK], F32)
    maskf = pb.sb("a0_mask", [128, 2, BLK], F32)

    qg.inc(qg.e.dma_start(out=rrow[:, :],
                          in_=slope_ap.rearrange("(a h) b c -> a (h b c)", a=1)))
    gp.inc(nc.gpsimd.iota(iotac[:], pattern=[[1, 1]], base=0,
                          channel_multiplier=1,
                          allow_small_or_imprecise_dtypes=True))
    for sc in range(2):
        gp.chain()
        gp.inc(nc.gpsimd.iota(deltp[:, sc, :], pattern=[[1, BLK]],
                              base=-128 * sc, channel_multiplier=-1,
                              allow_small_or_imprecise_dtypes=True))
    gp.chain()
    dv.wait(gp)
    dv.inc(nc.vector.tensor_scalar(out=caus.ap()[0:128, :, :],
                                   in0=deltp.ap()[0:128, :, :], scalar1=0.0,
                                   scalar2=None, op0=ALU.is_ge))
    dv.chain()
    dv.inc(nc.vector.tensor_scalar(out=deltp.ap()[0:128, :, :],
                                   in0=deltp.ap()[0:128, :, :], scalar1=0.0,
                                   scalar2=None, op0=ALU.max))
    dv.chain()
    pe.wait(dv)
    pe.wait(qg)
    pe.inc(nc.tensor.matmul(ps8[:, 6, 0:16], pb.onesrow[0:1, :], rrow[0:1, :],
                            start=True, stop=True))
    dv.wait(pe)
    dv.inc(nc.vector.tensor_copy(out=rbc[:, :], in_=ps8[:, 6, 0:16]))
    dv.chain()
    dv.inc(nc.vector.tensor_scalar(out=bdc[:, :], in0=rbc[:, :],
                                   scalar1=-256.0, scalar2=None, op0=ALU.mult))
    dv.chain()
    ac.wait(dv)
    ac.inc(nc.scalar.activation(out=bdc[:, :], in_=bdc[:, :], func=AF.Exp))
    ac.chain()
    nrbc = pb.sb("a0_nrbc", [128, 16], F32)
    dv.inc(nc.vector.tensor_scalar(out=nrbc[:, :], in0=rbc[:, :],
                                   scalar1=-1.0, scalar2=None, op0=ALU.mult))
    dv.chain()
    pb.bar()

    rt = pb.sb("a0_rt", [128, D], F32)
    tmA = pb.sb("a0_tmA", [128, D], F32)
    tmB = pb.sb("a0_tmB", [128, D], F32)
    xnf = pb.sb("a0_xnf", [128, NDC, 512], BF16)
    wbuf = pb.sb("a0_wbuf", [128, 2, NDC, 512], BF16)
    stg = pb.sb("a0_stg", [128, 4, 512], F32)
    cols = pb.sb("a0_cols", [128, 16], F32)

    spans = list(range(4, 12)) if prefix_only else list(range(16))
    NSP = len(spans)
    qs.inc(qs.e.dma_start(out=wbuf[:, 0, :, :],
                          in_=wqkT_ap[:, :, spans[0] * 512:spans[0] * 512 + 512]))
    qsb.inc(qsb.e.dma_start(out=wbuf[:, 1, :, :],
                            in_=wqkT_ap[:, :, spans[1] * 512:spans[1] * 512 + 512]))
    lo = spans[0] * 512
    RT = 128 * D
    rctrA = pb.counter(qg, "ra0", 0, RT, cap=(NT // 128 - 1) * RT)
    SWA = NSP * 512
    octrA = [pb.counter(qgb, f"qa{i}", 4 * i * 65536, NSP * 65536,
                        cap=((NT // 128 - 1) * NSP + 4 * i) * 65536)
             for i in range(NSP // 4)]



    with pb.loop(NT // 128, key="a0_A" + ("P" if prefix_only else "")) as ti:
        qg.wait(qg)
        qg.wait(dv)
        qg.inc(qg.e.dma_start(
            out=rt[:, :],
            in_=bass.AP(Rin_ap.tensor, rctrA.use(), [[D, 128], [1, D]])))
        rctrA.adv()
        dv.wait(qg)
        dv.wait(pe)
        ln_tile(pb, rt[:, :], tmA[:, :], tmB[:, :], cols, 0)
        transpose_f32_to_bf16(
            pb, lambda c: tmA[:, c * 128:(c + 1) * 128],
            lambda c0, w: xnf.ap()[0:128, c0:c0 + w, 0:128],
            NDC, ps8[:, 6, :])
        for sp in range(NSP):
            par = sp % 2
            qw = qs if par == 0 else qsb
            pe.wait(qw)
            pe.wait(dv)
            pe.wait(ac)
            for dc in range(NDC):
                mm = nc.tensor.matmul(
                    ps8[:, sp % 4, :], xnf[:, dc, 0:128],
                    wbuf[:, par, dc, :],
                    start=(dc == 0), stop=(dc == NDC - 1))
                if dc == NDC - 1:
                    pe.inc(mm)
            qw.wait(pe)
            qw.wait(qw)
            nsp2 = (sp + 2) % NSP
            qw.inc(qw.e.dma_start(
                out=wbuf[:, par, :, :],
                in_=wqkT_ap[:, :, lo + nsp2 * 512:lo + nsp2 * 512 + 512]))
            gspan = spans[sp]
            dst = stg[:, sp % 4, :]
            if gspan < 8:
                ac.wait(pe)
                ac.wait(qgb)
                ac.inc(nc.scalar.activation(out=dst, in_=ps8[:, sp % 4, :],
                                            func=AF.Silu))
                ac.chain()
            else:
                dv.wait(pe)
                dv.wait(qgb)
                dv.inc(nc.vector.tensor_copy(out=dst, in_=ps8[:, sp % 4, :]))
                dv.chain()
            if sp % 4 == 3:
                qgb.wait(ac)
                qgb.wait(dv)
                qgb.inc(qgb.e.dma_start(
                    out=bass.AP(qkvu_ap.tensor, octrA[sp // 4].use(),
                                [[512, 128], [65536, 4], [1, 512]]),
                    in_=stg[:, :, :]))
                octrA[sp // 4].adv()
    pb.bar()

    SW = NSP * 512
    koff = 0 if not prefix_only else -2048
    qb = pb.sb("a0_qb", [128, 2, SW], F32)
    kvst = pb.sb("a0_kvst", [128, NH, HD], F32)
    kv16 = pb.sb("a0_kv16", [128, HD], BF16)
    kt16 = pb.sb("a0_kt16", [128, 2, D], BF16)
    vt16 = pb.sb("a0_vt16", [128, 2, D], BF16)
    kf16 = pb.sb("a0_kf16", [128, 2, HD], BF16)
    qf16 = pb.sb("a0_qf16", [128, 2, HD], BF16)
    sm16 = pb.sb("a0_sm16", [128, 2, BLK], BF16)
    obig = pb.sb("a0_obig", [128, 2, D], F32)
    ouf = pb.sb("a0_ouf", [128, NDC, BLK], BF16)

    qg.wait(qgb)
    qg.inc(qg.e.dma_start(out=kvst.ap()[0:128, :, :], in_=kvmid_ap[:, :].rearrange(
        "p (h e) -> p h e", h=NH)))
    QB = 128 * SW
    bctr = pb.counter(qg, "bb0", 0, 2 * QB, cap=(NT // 128 - 2) * QB)
    rctrB = [pb.counter(qg, "rb0", 0, 2 * RT, cap=(NT // 128 - 2) * RT),
             pb.counter(qg, "rb1", RT, 2 * RT, cap=(NT // 128 - 1) * RT)]
    octrB = [pb.counter(qgb, "ob0", 0, 2 * RT, cap=(NT // 128 - 2) * RT),
             pb.counter(qgb, "ob1", RT, 2 * RT, cap=(NT // 128 - 1) * RT)]

    with pb.loop(NB, key="a0_B" + ("P" if prefix_only else "")) as blk:
        qg.wait(qg)
        qg.wait(dv)
        qg.wait(pe)
        qg.inc(qg.e.dma_start(
            out=qb[:, :, :],
            in_=bass.AP(qkvu_ap.tensor, bctr.use(),
                        [[512, 128], [NSP * 65536, 2], [65536, NSP],
                         [1, 512]])))
        bctr.adv()
        dv.wait(qg)
        dv.wait(qgb)
        for tc in range(2):
            dv.inc(nc.vector.tensor_copy(
                out=vt16[:, tc, :], in_=qb[:, tc, 4096 + koff:6144 + koff]))
            dv.chain()
        for h in range(NH):
            hs = h * HD
            dv.wait(ac)
            for tc in range(2):
                dv.inc(nc.vector.tensor_scalar(
                    out=biasc[:, tc:tc + 1], in0=rbc[:, h:h + 1],
                    scalar1=-(1.0 + 128.0 * tc), scalar2=None, op0=ALU.mult))
                dv.chain()
                dv.inc(nc.vector.tensor_scalar(
                    out=biasc[:, 2 + tc:3 + tc], in0=rbc[:, h:h + 1],
                    scalar1=-(255.0 - 128.0 * tc), scalar2=None, op0=ALU.mult))
                dv.chain()
            if not prefix_only:
                for tc in range(2):
                    ac.wait(dv)
                    ac.inc(nc.scalar.activation(
                        out=cols[:, 12 + tc:13 + tc], in_=iotac[:, :],
                        func=AF.Exp, scale=nrbc[:, h:h + 1],
                        bias=biasc[:, tc:tc + 1]))
                    ac.chain()
            for tc in range(2):
                ac.wait(dv)
                ac.inc(nc.scalar.activation(
                    out=cols[:, 14 + tc:15 + tc], in_=iotac[:, :],
                    func=AF.Exp, scale=rbc[:, h:h + 1],
                    bias=biasc[:, 2 + tc:3 + tc]))
                ac.chain()
            for tc in range(2):
                dv.wait(ac)
                dv.inc(nc.vector.tensor_scalar(
                    out=kt16[:, tc, hs:hs + HD],
                    in0=qb[:, tc, 2048 + koff + hs:2048 + koff + hs + HD],
                    scalar1=cols[:, 14 + tc:15 + tc], scalar2=None,
                    op0=ALU.mult))
                dv.chain()
            if not prefix_only:
                for tc in range(2):
                    pe.wait(dv)
                    tr = nc.tensor.transpose(
                        ps8[:, 5, 0:128],
                        qb[:, tc, 2048 + koff + hs:2048 + koff + hs + HD],
                        pb.ident32[:])
                    pe.inc(tr)
                    dv.wait(pe)
                    dv.inc(nc.vector.tensor_copy(out=kf16[:, tc, :],
                                                 in_=ps8[:, 5, 0:128]))
                    dv.chain()
                for tc in range(2):
                    pe.wait(dv)
                    pe.inc(nc.tensor.transpose(ps8[:, 5, 0:128],
                                               qb[:, tc, hs:hs + HD],
                                               pb.ident32[:]))
                    dv.wait(pe)
                    dv.inc(nc.vector.tensor_copy(out=qf16[:, tc, :],
                                                 in_=ps8[:, 5, 0:128]))
                    dv.chain()
                pe.wait(dv)
                for sc in range(2):
                    pe.inc(nc.tensor.matmul(
                        ps8[:, sc, 0:BLK], kf16[:, sc, :],
                        qf16.ap()[0:128, :, :].rearrange("p a b -> p (a b)"),
                        start=True, stop=True))
                for sc in range(2):
                    ac.wait(dv)
                    ac.inc(nc.scalar.activation(
                        out=maskf[:, sc, :], in_=deltp[:, sc, :], func=AF.Exp,
                        scale=nrbc[:, h:h + 1]))
                    ac.chain()
                dv.wait(ac)
                dv.inc(nc.vector.tensor_tensor(
                    out=maskf.ap()[0:128, :, :], in0=maskf.ap()[0:128, :, :],
                    in1=caus.ap()[0:128, :, :], op=ALU.mult))
                dv.chain()
                dv.wait(pe)
                dv.inc(nc.vector.tensor_tensor(
                    out=sm16[:, 0, :], in0=ps8[:, 0, 0:BLK],
                    in1=maskf[:, 0, :], op=ALU.mult))
                dv.chain()
                dv.inc(nc.vector.tensor_tensor(
                    out=sm16[:, 1, :], in0=ps8[:, 1, 0:BLK],
                    in1=maskf[:, 1, :], op=ALU.mult))
                dv.chain()
                dv.inc(nc.vector.tensor_copy(out=kv16[:, :],
                                             in_=kvst[:, h, :]))
                dv.chain()
                pe.wait(dv)
                for tc in range(2):
                    for sc in range(2):
                        mm = nc.tensor.matmul(
                            ps8[:, 2 + tc, 0:HD], sm16[:, sc,
                                                       tc * 128:(tc + 1) * 128],
                            vt16[:, sc, hs:hs + HD],
                            start=(sc == 0), stop=(sc == 1))
                        if sc == 1:
                            pe.inc(mm)
                    mm = nc.tensor.matmul(
                        ps8[:, 6 + tc, 0:HD], qf16[:, tc, :], kv16[:, :],
                        start=True, stop=True)
                    pe.inc(mm)
                dv.wait(pe)
                for tc in range(2):
                    dv.wait(ac)
                    dv.inc(nc.vector.tensor_scalar(
                        out=tmA[:, 0:HD], in0=ps8[:, 6 + tc, 0:HD],
                        scalar1=cols[:, 12 + tc:13 + tc], scalar2=None,
                        op0=ALU.mult))
                    dv.chain()
                    dv.inc(nc.vector.tensor_tensor(
                        out=obig[:, tc, hs:hs + HD], in0=tmA[:, 0:HD],
                        in1=ps8[:, 2 + tc, 0:HD], op=ALU.add))
                    dv.chain()
            pe.wait(dv)
            for sc in range(2):
                mm = nc.tensor.matmul(ps8[:, 4, 0:HD], kt16[:, sc, hs:hs + HD],
                                      vt16[:, sc, hs:hs + HD],
                                      start=(sc == 0), stop=(sc == 1))
                if sc == 1:
                    pe.inc(mm)
            dv.wait(pe)
            dv.inc(nc.vector.scalar_tensor_tensor(
                out=kvst[:, h, :], in0=kvst[:, h, :],
                scalar=bdc[:, h:h + 1], in1=ps8[:, 4, 0:HD],
                op0=ALU.mult, op1=ALU.add))
            dv.chain()
        if not prefix_only:
            for tc in range(2):
                am = cols[:, 4:5]
                ac.wait(dv)
                ac.inc(nc.scalar.activation(out=tmA[:, :], in_=obig[:, tc, :],
                                            func=AF.Square))
                ac.chain()
                dv.wait(ac)
                dv.inc(nc.vector.reduce_sum(out=am, in_=tmA[:, :], axis=AX))
                dv.chain()
                dv.inc(nc.vector.tensor_scalar(out=am, in0=am, scalar1=1.0 / D,
                                               scalar2=1e-6, op0=ALU.mult,
                                               op1=ALU.add))
                dv.chain()
                ac.wait(dv)
                ac.inc(nc.scalar.activation(out=am, in_=am, func=AF.Sqrt))
                ac.chain()
                dv.wait(ac)
                dv.inc(nc.vector.reciprocal(out=am, in_=am))
                dv.chain()
                dv.inc(nc.vector.scalar_tensor_tensor(
                    out=obig[:, tc, :], in0=obig[:, tc, :], scalar=am,
                    in1=qb[:, tc, 6144:8192], op0=ALU.mult, op1=ALU.mult))
                dv.chain()
                transpose_f32_to_bf16(
                    pb, lambda c: obig[:, tc, c * 128:(c + 1) * 128],
                    lambda c0, w: ouf.ap()[0:128, c0:c0 + w,
                                           tc * 128:(tc + 1) * 128],
                    NDC, ps8[:, 5, :])
            for sp in range(4):
                par = sp % 2
                qw = qs if par == 0 else qsb
                qw.wait(qw)
                qw.wait(pe)
                qw.inc(qw.e.dma_start(
                    out=wbuf[:, par, :, :],
                    in_=woT_ap[:, :, sp * 512:(sp + 1) * 512]))
                pe.wait(qw)
                pe.wait(dv)
                for tc in range(2):
                    for dc in range(NDC):
                        mm = nc.tensor.matmul(
                            ps8[:, tc * 4 + sp, :],
                            ouf[:, dc, tc * 128:(tc + 1) * 128],
                            wbuf[:, par, dc, :],
                            start=(dc == 0), stop=(dc == NDC - 1))
                        if dc == NDC - 1:
                            pe.inc(mm)
            for tc in range(2):
                qg.wait(qg)
                qg.wait(dv)
                qg.inc(qg.e.dma_start(
                    out=rt[:, :],
                    in_=bass.AP(Rin_ap.tensor, rctrB[tc].use(),
                                [[D, 128], [1, D]])))
                rctrB[tc].adv()
                dv.wait(qg)
                dv.wait(pe)
                dv.wait(qgb)
                for sp in range(4):
                    dv.inc(nc.vector.tensor_tensor(
                        out=obig[:, tc, sp * 512:(sp + 1) * 512],
                        in0=ps8[:, tc * 4 + sp, :],
                        in1=rt[:, sp * 512:(sp + 1) * 512], op=ALU.add))
                    dv.chain()
                qgb.wait(dv)
                qgb.inc(qgb.e.dma_start(
                    out=bass.AP(Rout_ap.tensor, octrB[tc].use(),
                                [[D, 128], [1, D]]),
                    in_=obig[:, tc, :]))
                octrB[tc].adv()
    qg.wait(dv)
    qg.inc(qg.e.dma_start(out=kvend_ap[:, :],
                          in_=kvst.ap()[0:128, :, :].rearrange(
                              "p h e -> p (h e)")))
    pb.bar()


def build_l0attn(registry=None, prefix_only=False, parts=None):
    pb = PB(registry)
    nc = pb.nc
    pb.onesrow = pb.sb("g_ones_r", [1, 128], F32)
    pb.onescol32 = pb.sb("g_ones_c", [128, 1], F32)
    Rin = nc.dram_tensor("Rin", [NT, D], F32, kind="ExternalInput").ap()
    Wqk = nc.dram_tensor("W_qkvu", [8192, D], F32, kind="ExternalInput").ap()
    Wout = nc.dram_tensor("W_out", [D, D], F32, kind="ExternalInput").ap()
    slope = nc.dram_tensor("slope_rate", [16, 1, 1], F32,
                           kind="ExternalInput").ap()
    kvmid = nc.dram_tensor("kvmid", [128, NHD * 128], F32,
                           kind="ExternalInput").ap()
    Rout = nc.dram_tensor("Rout", [NT, D], F32, kind="ExternalOutput").ap()
    kvend = nc.dram_tensor("kvend", [128, NHD * 128], F32,
                           kind="ExternalOutput").ap()
    SW = 4096 if prefix_only else 8192
    qkvu = nc.dram_tensor("qkvu", [NT // 128, SW // 512, 128, 512], F32,
                          kind="Internal").ap()
    wqkT = nc.dram_tensor("wqkT", [128, NDC, 8192], BF16, kind="Internal").ap()
    woT = nc.dram_tensor("woT", [128, NDC, D], BF16, kind="Internal").ap()

    setup_common(pb)
    gp = pb.gp
    gp.inc(nc.gpsimd.memset(pb.onesrow[:, :], 1.0))
    gp.inc(nc.gpsimd.memset(pb.onescol32[:, :], 1.0))
    pb.bar()
    if parts is None or "wq" in parts:
        emit_prep_rowquant(pb, Wqk, wqkT, 8192, D, plain_groups(8192),
                           key="pq", rowquant=False)
    if (parts is None or "wo" in parts) and not prefix_only:
        emit_prep_rowquant(pb, Wout, woT, D, D, plain_groups(D),
                           key="po", rowquant=False)
    if parts is None or "main" in parts:
        emit_l0attn(pb, Rin, Rout, slope, wqkT, woT, kvmid, kvend, qkvu,
                    prefix_only=prefix_only)
    else:
        pb.qg.inc(nc.gpsimd.dma_start(out=Rout[0:128, :], in_=Rin[0:128, :]))
        pb.qg.inc(nc.gpsimd.dma_start(out=kvend[:, :], in_=kvmid[:, :]))
        pb.bar()
    pb.es.close()
    return pb




def _shard_rows(x):
    out = []
    for c in range(NCORE):
        b, h = c // 2, c % 2
        out.append(np.ascontiguousarray(x[b, h * NT:(h + 1) * NT, :],
                                        dtype=np.float32))
    return out


_BUILD_CACHE = {}


def _get(builder, **kw):
    key = (builder.__name__, tuple(sorted(kw.items())))
    if key not in _BUILD_CACHE:
        _BUILD_CACHE[key] = build_twopass(builder, **kw)
    return _BUILD_CACHE[key]


def kernel(L0_attn_x, L0_attn_slope_rate, L0_attn_W_qkvu, L0_attn_W_out,
           L0_ffn_gate_up_packed, L0_ffn_gate_up_scales, L0_ffn_down_packed,
           L0_ffn_down_scales, L1_attn_Wi, L1_attn_Wf, L1_attn_Wg,
           L1_attn_Wo, L1_attn_norm_i, L1_attn_norm_f, L1_attn_norm_g,
           L1_attn_norm_o, L1_attn_g_norm_w, L1_ffn_w_gate_up,
           L1_ffn_w_down):
    x = np.asarray(L0_attn_x, dtype=np.float32)
    B, S, _ = x.shape
    shards = _shard_rows(x)
    f32 = lambda a: np.ascontiguousarray(np.asarray(a, dtype=np.float32))
    wq = f32(L0_attn_W_qkvu)
    wo0 = f32(L0_attn_W_out)
    sr = f32(L0_attn_slope_rate)
    gup8 = np.ascontiguousarray(
        np.asarray(L0_ffn_gate_up_packed, dtype=np.int32)).view(np.uint8)
    gus = np.ascontiguousarray(np.asarray(L0_ffn_gate_up_scales,
                                          dtype=np.float16))
    dnp8 = np.ascontiguousarray(
        np.asarray(L0_ffn_down_packed, dtype=np.int32)).view(np.uint8)
    dns = np.ascontiguousarray(np.asarray(L0_ffn_down_scales,
                                          dtype=np.float16))
    Wi, Wf, Wg, Wo = (f32(L1_attn_Wi), f32(L1_attn_Wf), f32(L1_attn_Wg),
                      f32(L1_attn_Wo))
    ni, nf, ng, no, gnw = (f32(L1_attn_norm_i), f32(L1_attn_norm_f),
                           f32(L1_attn_norm_g), f32(L1_attn_norm_o),
                           f32(L1_attn_g_norm_w))
    w1 = f32(L1_ffn_w_gate_up)
    w2 = f32(L1_ffn_w_down)
    zkv = np.zeros((128, 2048), dtype=np.float32)
    zh = np.zeros((128, 16), dtype=np.float32)
    cores = list(range(NCORE))

    pb = _get(build_l0attn, prefix_only=True)
    res = run_bass_kernel_spmd(pb.nc, [
        dict(Rin=shards[c], W_qkvu=wq, W_out=wo0, slope_rate=sr, kvmid=zkv)
        for c in cores], core_ids=cores)
    kvend = [res.results[c]["kvend"] for c in cores]

    pb = _get(build_l0attn)
    res = run_bass_kernel_spmd(pb.nc, [
        dict(Rin=shards[c], W_qkvu=wq, W_out=wo0, slope_rate=sr,
             kvmid=(zkv if c % 2 == 0 else np.ascontiguousarray(kvend[c - 1])))
        for c in cores], core_ids=cores)
    R = [res.results[c]["Rout"] for c in cores]

    pb = _get(build_l0ffn)
    res = run_bass_kernel_spmd(pb.nc, [
        dict(Rin=R[c], gu_packed_u8=gup8, gu_scales=gus,
             dn_packed_u8=dnp8, dn_scales=dns)
        for c in cores], core_ids=cores)
    R = [res.results[c]["Rout"] for c in cores]

    pb = _get(build_l1attn, prefix_only=True)
    res = run_bass_kernel_spmd(pb.nc, [
        dict(Rin=R[c], Wi=Wi, Wf=Wf, Wg=Wg, Wo=Wo, norm_i=ni, norm_f=nf,
             norm_g=ng, norm_o=no, g_norm_w=gnw, hmid=zh)
        for c in cores], core_ids=cores)
    hend = [res.results[c]["hend"] for c in cores]

    pb = _get(build_l1attn)
    res = run_bass_kernel_spmd(pb.nc, [
        dict(Rin=R[c], Wi=Wi, Wf=Wf, Wg=Wg, Wo=Wo, norm_i=ni, norm_f=nf,
             norm_g=ng, norm_o=no, g_norm_w=gnw,
             hmid=(zh if c % 2 == 0 else np.ascontiguousarray(hend[c - 1])))
        for c in cores], core_ids=cores)
    R = [res.results[c]["Rout"] for c in cores]

    pb = _get(build_l1ffn)
    res = run_bass_kernel_spmd(pb.nc, [
        dict(Rin=R[c], w_gate_up=w1, w_down=w2)
        for c in cores], core_ids=cores)
    R = [res.results[c]["Rout"] for c in cores]

    out = np.empty((B, S, D), dtype=np.float32)
    for c in range(NCORE):
        b, h = c // 2, c % 2
        out[b, h * NT:(h + 1) * NT, :] = R[c]
    return out


# version 02
# speedup vs baseline: 1.1398x; 1.1398x over previous
import contextlib
import math
import numpy as np
import concourse.bass as bass
import concourse.mybir as mybir
from concourse.bass_utils import run_bass_kernel_spmd

F32 = mybir.dt.float32
BF16 = mybir.dt.bfloat16
I32 = mybir.dt.int32
U8 = mybir.dt.uint8
F16 = mybir.dt.float16
AF = mybir.ActivationFunctionType
ALU = mybir.AluOpType
AX = mybir.AxisListType.X
TS = bass.ts
DS = bass.ds
ET = mybir.EngineType

MAGIC = 1.5 * 2.0 ** 23
LN2 = math.log(2.0)

NCORE = 8
NT = 4096
NHD = 16
D = 2048
DI = 5632
NDC = D // 128
NIC = DI // 128


class Eng:

    def __init__(self, pb, engine, name, step=1):
        self.pb = pb
        self.e = engine
        self.name = name
        self.step = step
        self.s = pb.es.enter_context(pb.nc.semaphore(name + "_sem"))
        self.base = 0
        self.stack = []

    def inc(self, inst):
        inst.then_inc(self.s, self.step)
        self._bump(self.step)
        return inst

    def _bump(self, n):
        if self.stack:
            self.stack[-1][1] += n
            self.stack[-1][2] += n
        else:
            self.base += n

    def ktot(self):
        return sum(kr for (_, kt, kr) in self.stack)

    def wait(self, other, back=0):
        pb = self.pb
        if not pb.loopstack:
            self.e.wait_ge(other.s, other.base - back)
            return
        pair = (self.name, other.name)
        pb.all_pairs.add(pair)
        kconst = other.ktot() - back
        assert kconst + other.base >= 0
        rb = pb.pair_reg(self, other)
        rt = pb.tmp_reg(self)
        self.e.reg_alu(rt, rb, kconst, op=ALU.add)
        self.e.wait_ge(other.s, rt)

    def chain(self):
        self.wait(self)

    def wait_of(self, eng_other):
        self.wait(eng_other)

    def drain(self):
        self.wait(self)


class IdxCounter:
    def __init__(self, pb, q, name, hi):
        self.pb, self.q = pb, q
        self.step, self.wrap, self.cap = 1, None, None
        self.r = pb.reg(q, name)
        self.lo, self.hi = 0, hi
        self._sv = q.e.snap(self.r, min_val=0, max_val=hi, donate=True)

    def reinit(self, init, step, wrap, cap):
        self.step, self.wrap, self.cap = step, wrap, cap
        self.q.e.reg_mov(self.r, init)
        return self

    def use(self):
        return self._sv

    def adv(self):
        e = self.q.e
        e.reg_alu(self.r, self.r, self.step, op=ALU.add)
        if self.wrap is not None:
            e.reg_alu(self.r, self.r, self.wrap, op=ALU.mod)
        elif self.cap is not None:
            e.reg_alu(self.r, self.r, self.cap, op=ALU.min)


class PB:
    def __init__(self, registry=None):
        self.nc = bass.Bass("TRN2", target_bir_lowering=False, debug=False)
        self.es = contextlib.ExitStack()
        self.registry = registry if registry is not None else {}
        self.new_registry = {}
        nc = self.nc
        self.pe = Eng(self, nc.tensor, "pe")
        self.dv = Eng(self, nc.vector, "dv")
        self.ac = Eng(self, nc.scalar, "ac")
        self.gp = Eng(self, nc.gpsimd, "gp")
        self.qs = Eng(self, nc.sync, "qs", step=16)
        self.qsb = Eng(self, nc.sync, "qsb", step=16)
        self.qg = Eng(self, nc.gpsimd, "qg", step=16)
        self.qgb = Eng(self, nc.gpsimd, "qgb", step=16)
        self.engines = [self.pe, self.dv, self.ac, self.gp,
                        self.qs, self.qsb, self.qg, self.qgb]
        self.byname = {e.name: e for e in self.engines}
        self.loopstack = []
        self.all_pairs = set(map(tuple, (registry or {}).get("__pairs__", [])))
        self.ident32 = self.sb("g_ident32", [128, 128], F32)
        self.ps8 = self.psum("g_ps8", [128, 8, 512], F32)
        self.regs = {}
        self.constvals = [-7.5, LN2, -LN2, 1.0, 0.5, -0.5, 2.0, 1e-5, 1e-6]
        self.consts = self.sb("g_consts", [128, len(self.constvals)], F32)
        for k, v in enumerate(self.constvals):
            self.nc.const_aps.register(F32, v, self.consts[:, k:k + 1])

    def reg(self, eng, name):
        key = (eng.name, name)
        if key not in self.regs:
            self.regs[key] = eng.e.alloc_register(name)
        return self.regs[key]

    def sb(self, name, shape, dt=F32):
        return self.es.enter_context(self.nc.sbuf_tensor(name, shape, dt))

    def psum(self, name, shape, dt=F32):
        return self.es.enter_context(self.nc.psum_tensor(name, shape, dt))

    def bar(self):
        for q in (self.qs, self.qsb, self.qg, self.qgb):
            q.drain()
        self.nc.all_engine_barrier()

    @contextlib.contextmanager
    def loop(self, n, key):
        assert n >= 1
        reg = self.registry.get(key, {"deltas": {}})
        rec = {"deltas": {},
               "inner": {e.name: 0 for e in self.engines}}
        if not self.loopstack:
            for (wn, pn) in sorted(self.all_pairs):
                w = self.byname[wn]
                p = self.byname[pn]
                w.e.reg_mov(self.pair_reg(w, p), p.base)
        self.loopstack.append(rec)
        with self.nc.Fori(0, n) as i:
            for e in self.engines:
                e.stack.append([rec, 0, 0])
            yield i
            deltas = {}
            for e in self.engines:
                deltas[e.name] = e.stack[-1][1]
            for (wn, pn) in sorted(self.all_pairs):
                w = self.byname[wn]
                resid = deltas[pn] - rec["inner"][pn]
                if resid != 0:
                    w.e.reg_alu(self.pair_reg(w, self.byname[pn]),
                                self.pair_reg(w, self.byname[pn]),
                                resid, op=ALU.add)
            for e in self.engines:
                _, kt, kr = e.stack.pop()
                if key in self.registry:
                    d = reg["deltas"].get(e.name, 0)
                    assert kt == d, (f"loop {key} delta mismatch {e.name}: "
                                     f"registered {d}, emitted {kt}")
                add = n * kt
                if e.stack:
                    e.stack[-1][1] += add
                else:
                    e.base += add
            self.loopstack.pop()
            if self.loopstack:
                parent = self.loopstack[-1]
                for e in self.engines:
                    parent["inner"][e.name] += n * deltas[e.name]
        self.new_registry[key] = {
            "deltas": {k: v for k, v in deltas.items() if v},
        }
        self.new_registry["__pairs__"] = sorted(self.all_pairs)

    def pair_reg(self, w, p):
        key = ("pair", w.name, p.name)
        if key not in self.regs:
            self.regs[key] = w.e.alloc_register(f"rb_{w.name}_{p.name}")
        return self.regs[key]

    def tmp_reg(self, w):
        key = ("tmp", w.name)
        if key not in self.regs:
            self.regs[key] = w.e.alloc_register(f"rt_{w.name}")
        return self.regs[key]

    def counter(self, q, name, init, step, wrap=None, cap=None, maxv=None):
        hi = maxv if maxv is not None else (
            (wrap - 1) if wrap is not None else cap)
        key = ("ctr", q.name, name, hi)
        if key not in self.regs:
            self.regs[key] = IdxCounter(self, q, name + f"_{hi}", hi)
        return self.regs[key].reinit(init, step, wrap, cap)




def A(x):
    return x if isinstance(x, bass.AP) else x[:]

def setup_common(pb):
    nc, gp = pb.nc, pb.gp
    for k, v in enumerate(pb.constvals):
        gp.inc(nc.gpsimd.memset(pb.consts[:, k:k + 1], v))
    gp.chain()
    gp.inc(nc.gpsimd.memset(pb.ident32[:], 0.0))
    gp.chain()
    gp.inc(nc.gpsimd.affine_select(
        out=pb.ident32[:], in_=pb.ident32[:], compare_op=ALU.not_equal,
        fill=1.0, base=0, pattern=[[-1, 128]], channel_multiplier=1))
    gp.chain()
    pb.bar()




def ln_tile(pb, x_ap, xn_ap, tmp_ap, cols, ci):
    nc, dv, ac = pb.nc, pb.dv, pb.ac
    x_ap, xn_ap, tmp_ap = A(x_ap), A(xn_ap), A(tmp_ap)
    m = cols[:, ci:ci + 1]
    v = cols[:, ci + 1:ci + 2]
    r = cols[:, ci + 2:ci + 3]
    dv.inc(nc.vector.reduce_sum(out=m, in_=x_ap, axis=AX))
    dv.chain()
    dv.inc(nc.vector.tensor_scalar(out=m, in0=m, scalar1=1.0 / D, scalar2=None,
                                   op0=ALU.mult))
    dv.chain()
    dv.inc(nc.vector.tensor_scalar(out=tmp_ap, in0=x_ap, scalar1=m,
                                   scalar2=None, op0=ALU.subtract))
    dv.chain()
    ac.wait(dv)
    ac.inc(nc.scalar.activation(out=xn_ap, in_=tmp_ap, func=AF.Square,
                                accum_out=v))
    ac.chain()
    ac.inc(nc.scalar.activation(out=r, in_=v, func=AF.Sqrt,
                                scale=1.0 / D, bias=1e-5))
    ac.chain()
    dv.wait(ac)
    dv.inc(nc.vector.reciprocal(out=r, in_=r))
    dv.chain()
    dv.inc(nc.vector.tensor_scalar(out=xn_ap, in0=tmp_ap, scalar1=r,
                                   scalar2=None, op0=ALU.mult))
    dv.chain()


def emit_actquant(pb, x_ap, q_ap, cols, ci, inv_qmax=1.0 / 127.0, eps=1e-8):
    nc, dv = pb.nc, pb.dv
    x_ap, q_ap = A(x_ap), A(q_ap)
    am = cols[:, ci:ci + 1]
    sc = cols[:, ci + 1:ci + 2]
    rc = cols[:, ci + 2:ci + 3]
    dv.inc(nc.vector.reduce_max(out=am, in_=x_ap, axis=AX,
                                apply_absolute_value=True))
    dv.chain()
    dv.inc(nc.vector.tensor_scalar(out=sc, in0=am, scalar1=inv_qmax,
                                   scalar2=eps, op0=ALU.mult, op1=ALU.max))
    dv.chain()
    dv.inc(nc.vector.reciprocal(out=rc, in_=sc))
    dv.chain()
    dv.inc(nc.vector.tensor_scalar(out=q_ap, in0=x_ap, scalar1=rc,
                                   scalar2=MAGIC, op0=ALU.mult, op1=ALU.add))
    dv.chain()
    dv.inc(nc.vector.tensor_scalar(out=q_ap, in0=q_ap, scalar1=MAGIC,
                                   scalar2=None, op0=ALU.subtract))
    dv.chain()


def transpose_f32_to_bf16(pb, src_of, dst_of, nchunk, bank):
    nc, pe, dv = pb.nc, pb.pe, pb.dv
    for c0 in range(0, nchunk, 4):
        w = min(4, nchunk - c0)
        pe.wait(dv)
        for c in range(c0, c0 + w):
            tr = nc.tensor.transpose(bank[:, (c - c0) * 128:(c - c0 + 1) * 128],
                                     src_of(c), pb.ident32[:])
            if c == c0 + w - 1:
                pe.inc(tr)
        dv.wait(pe)
        dv.inc(nc.vector.tensor_copy(
            out=dst_of(c0, w),
            in_=bank[:, 0:w * 128].rearrange("p (a b) -> p a b", b=128)))
        dv.chain()




def emit_prep_rowquant(pb, w_ap, wT_ap, O, K, groups, qmax=127.0,
                       eps=1e-8, key="prep", rowquant=True):
    nc = pb.nc
    dv, pe = pb.dv, pb.pe
    qs, qsb, qg = pb.qs, pb.qsb, pb.qg
    KC = K // 128
    w3 = w_ap.rearrange("(n p) k -> n p k", p=128)

    with contextlib.ExitStack() as es2:
        pbuf = es2.enter_context(nc.sbuf_tensor(key + "_pbuf", [128, 2, K], F32))
        wq16 = es2.enter_context(nc.sbuf_tensor(key + "_wq", [128, K], F32))
        tq = es2.enter_context(nc.sbuf_tensor(key + "_tq", [128, KC, 128], BF16))
        ccols = es2.enter_context(nc.sbuf_tensor(key + "_cc", [128, 4], F32))
        bank = pb.ps8[:, 0, :]

        gsz = len(groups[0])
        ngrp = len(groups)
        base = [groups[0][j] for j in range(gsz)]
        step = ([(groups[1][j][0] - groups[0][j][0],
                  groups[1][j][1] - groups[0][j][1]) for j in range(gsz)]
                if ngrp > 1 else [(0, 0)] * gsz)
        for g in range(ngrp):
            for j in range(gsz):
                assert groups[g][j] == (base[j][0] + step[j][0] * g,
                                        base[j][1] + step[j][1] * g), (g, j)

        for j in range(min(gsz, 2)):
            q = qs if j % 2 == 0 else qsb
            q.inc(q.e.dma_start(out=pbuf[:, j % 2, :],
                                in_=w3[base[j][0], :, :]))

        TSZ = 128 * K
        cmax = (O // 128 - 1) * TSZ
        ctr = []
        for j in range(gsz):
            nxt = j + 2
            q = qs if j % 2 == 0 else qsb
            if nxt < gsz:
                ctr.append(pb.counter(q, f"ctr{j}", base[nxt][0] * TSZ,
                                      step[nxt][0] * TSZ, cap=cmax))
            else:
                sl = nxt - gsz
                ctr.append(pb.counter(q, f"ctr{j}",
                                      (base[sl][0] + step[sl][0]) * TSZ,
                                      step[sl][0] * TSZ, cap=cmax))
        dctr = [pb.counter(pb.qg, f"dtr{j}", base[j][1], step[j][1],
                           cap=base[j][1] + step[j][1] * max(ngrp - 1, 1))
                for j in range(gsz)]

        with pb.loop(ngrp, key=key) as g:
            for j in range(gsz):
                par = j % 2
                q = qs if par == 0 else qsb
                dv.wait(q)
                dv.wait(qg)
                dv.wait(pe)
                src = pbuf[:, par, :]
                if rowquant:
                    emit_actquant(pb, src, wq16[:, :], ccols, 0, 1.0 / qmax, eps)
                    dv.inc(nc.vector.tensor_scalar(
                        out=wq16[:, :], in0=wq16[:, :], scalar1=ccols[:, 1:2],
                        scalar2=None, op0=ALU.mult))
                    dv.chain()
                else:
                    dv.inc(nc.vector.tensor_copy(out=wq16[:, :], in_=src))
                    dv.chain()
                q.wait(dv)
                q.wait(q)
                q.inc(q.e.dma_start(
                    out=pbuf[:, par, :],
                    in_=bass.AP(w_ap.tensor, ctr[j].use(),
                                [[K, 128], [1, K]])))
                ctr[j].adv()
                transpose_f32_to_bf16(
                    pb, lambda c: wq16[:, c * 128:(c + 1) * 128],
                    lambda c0, w: tq[:, c0:c0 + w, :],
                    KC, bank)
                qg.wait(dv)
                qg.inc(qg.e.dma_start(
                    out=wT_ap[:, :, DS(dctr[j].use(), 128)],
                    in_=tq[:, :, :]))
                dctr[j].adv()
        pb.bar()


def gu_pair_groups(I=DI, gran=256):
    tpg = gran // 128
    groups = []
    for q in range(I // gran):
        g = []
        for t in range(tpg):
            g.append((q * tpg + t, q * 2 * gran + t * 128))
        for t in range(tpg):
            g.append((I // 128 + q * tpg + t, q * 2 * gran + gran + t * 128))
        groups.append(g)
    return groups


def plain_groups(O):
    return [[(2 * g, 2 * g * 128), (2 * g + 1, (2 * g + 1) * 128)]
            for g in range(O // 256)]




def emit_l1ffn_main(pb, Rin_ap, Rout_ap, w1T_ap, w2T_ap, upto='full', mode='l1'):
    nc = pb.nc
    dv, ac, pe, gp = pb.dv, pb.ac, pb.pe, pb.gp
    qs, qsb, qg, qgb = pb.qs, pb.qsb, pb.qg, pb.qgb
    NSPAN = 22
    NGRP = 16
    ps8 = pb.ps8

    rgrp = pb.sb("f1_rgrp", [128, 2, D], F32)
    tmA = pb.sb("f1_tmA", [128, D], F32)
    tmB = pb.sb("f1_tmB", [128, D], F32)
    e256 = pb.sb("f1_e256", [128, 2, 2, 256], F32)
    u256 = pb.sb("f1_u256", [128, 2, 2, 256], F32)
    xqf = pb.sb("f1_xqf", [128, NDC, 256], BF16)
    wbuf = pb.sb("f1_wbuf", [128, 2, NDC, 512], BF16)
    hbig = pb.sb("f1_h", [128, 2, DI], F32)
    hqf = pb.sb("f1_hqf", [128, NIC, 256], BF16)
    w2buf = pb.sb("f1_w2", [128, 2, D], BF16)
    rres = pb.sb("f1_rres", [128, 2, D], F32)
    ccol = pb.sb("f1_cols", [128, 16], F32)

    Rin3 = Rin_ap.rearrange("(a p) d -> p a d", p=128)
    Rout3 = Rout_ap.rearrange("(a p) d -> p a d", p=128)

    qg.inc(qg.e.dma_start(out=rgrp[:, 0:1, :], in_=Rin3[:, 0:1, :]))
    qg.inc(qg.e.dma_start(out=rgrp[:, 1:2, :], in_=Rin3[:, 1:2, :]))
    qs.inc(qs.e.dma_start(out=wbuf[:, 0, :, :], in_=w1T_ap[:, :, 0:512]))
    qsb.inc(qsb.e.dma_start(out=wbuf[:, 1, :, :], in_=w1T_ap[:, :, 512:1024]))
    qg.inc(qg.e.dma_start(out=w2buf[:, 0, :],
                          in_=w2T_ap[:, 0:1, :].rearrange("p a b -> p (a b)")))
    qgb.inc(qgb.e.dma_start(out=w2buf[:, 1, :],
                            in_=w2T_ap[:, 1:2, :].rearrange("p a b -> p (a b)")))
    wctr = [pb.counter(qs, "ctr0", 2 * 512, 1024, wrap=NSPAN * 512,
                       maxv=(NSPAN - 1) * 512),
            pb.counter(qsb, "ctr1", 3 * 512, 1024, wrap=NSPAN * 512,
                       maxv=(NSPAN - 1) * 512)]
    RT = 128 * D
    rctr = [pb.counter(qg, "dtr0", 2 * RT, 2 * RT, cap=(2 * NGRP - 1) * RT),
            pb.counter(qg, "dtr1", 3 * RT, 2 * RT, cap=(2 * NGRP - 1) * RT)]
    octr = pb.counter(qgb, "dtr2", 0, 2 * RT, cap=(2 * NGRP - 2) * RT)

    with pb.loop(NGRP, key="f1_main") as g:
        dv.wait(qg)
        dv.wait(pe)
        for t in range(2):
            ln_tile(pb, rgrp[:, t, :], tmA[:, :], tmB[:, :], ccol, 0)
            if mode == 'l1':
                emit_actquant(pb, tmA, tmB, ccol, 4 + 4 * t)
            else:
                tmB, tmA = tmA, tmB
            transpose_f32_to_bf16(
                pb, lambda c: tmB[:, c * 128:(c + 1) * 128],
                lambda c0, w: xqf.ap()[0:128, c0:c0 + w,
                                       t * 128:(t + 1) * 128],
                NDC, ps8[:, 2, :])
            if mode == 'l0':
                tmB, tmA = tmA, tmB
        if upto == "ln":
            dv.inc(nc.vector.tensor_copy(out=rres[:, 0, :], in_=tmA[:, :]))
            dv.chain()
            dv.inc(nc.vector.tensor_copy(out=rres[:, 1, :], in_=tmB[:, :]))
            dv.chain()
        gu_on = upto in ("gu", "hq", "full")
        if gu_on:
         pe.wait(dv)
         with pb.loop(11, key="f1_gu") as p2:
            for sub in range(2):
                par = sub
                qw = qs if par == 0 else qsb
                pe.wait(qw)
                pe.wait(ac, back=4)
                for t in range(2):
                    for c in range(NDC):
                        mm = nc.tensor.matmul(
                            ps8[:, t + 4 * sub, :],
                            xqf[:, c, t * 128:(t + 1) * 128],
                            wbuf[:, par, c, :],
                            start=(c == 0), stop=(c == NDC - 1))
                        if c == NDC - 1:
                            pe.inc(mm)
                qw.wait(pe)
                qw.wait(qw)
                qw.inc(qw.e.dma_start(out=wbuf[:, par, :, :],
                                      in_=w1T_ap[:, :, DS(wctr[sub].use(), 512)]))
                wctr[sub].adv()
                ac.wait(pe)
                ac.wait(dv, back=2)
                for t in range(2):
                    sc = ccol[:, 5 + t * 4:6 + t * 4]
                    if mode == 'l1':
                        ac.inc(nc.scalar.activation(
                            out=e256[:, t, sub, :],
                            in_=ps8[:, t + 4 * sub, 0:256],
                            func=AF.Silu, scale=sc))
                        ac.inc(nc.scalar.activation(
                            out=u256[:, t, sub, :],
                            in_=ps8[:, t + 4 * sub, 256:512],
                            func=AF.Copy, scale=sc))
                    else:
                        ac.inc(nc.scalar.activation(
                            out=e256[:, t, sub, :],
                            in_=ps8[:, t + 4 * sub, 0:256],
                            func=AF.Silu))
                        ac.inc(nc.scalar.activation(
                            out=u256[:, t, sub, :],
                            in_=ps8[:, t + 4 * sub, 256:512],
                            func=AF.Copy))
                dv.wait(ac)
                for t in range(2):
                    dv.inc(nc.vector.tensor_tensor(
                        out=hbig.ap()[0:128, t, DS(p2 * 512 + sub * 256, 256)],
                        in0=e256[:, t, sub, :], in1=u256[:, t, sub, :],
                        op=ALU.mult))
                    dv.chain()
        if upto == "gu":
            dv.inc(nc.vector.tensor_copy(out=rres[:, 0, :],
                                         in_=hbig[:, 0, 0:D]))
            dv.chain()
            dv.inc(nc.vector.tensor_copy(out=rres[:, 1, :],
                                         in_=hbig[:, 1, 0:D]))
            dv.chain()
        if upto in ("hq", "full"):
         for t in range(2):
            if mode == 'l1':
                emit_actquant(pb, hbig[:, t, :], hbig[:, t, :], ccol, 8 + 4 * t)
            transpose_f32_to_bf16(
                pb, lambda c: hbig[:, t, c * 128:(c + 1) * 128],
                lambda c0, w: hqf.ap()[0:128, c0:c0 + w,
                                       t * 128:(t + 1) * 128],
                NIC, ps8[:, 2, :])
         if upto == "hq":
            dv.inc(nc.vector.tensor_copy(out=rres[:, 0, :],
                                         in_=hbig[:, 0, 0:D]))
            dv.chain()
            dv.inc(nc.vector.tensor_copy(out=rres[:, 1, :],
                                         in_=hbig[:, 1, 0:D]))
            dv.chain()
        if upto == "full":
         for icl in range(NIC):
            par = icl % 2
            qw = qg if par == 0 else qgb
            pe.wait(qw)
            if icl == 0:
                pe.wait(dv)
                pe.wait(ac)
            for t in range(2):
                for og in range(4):
                    mm = nc.tensor.matmul(
                        ps8[:, t * 4 + og, :],
                        hqf[:, icl, t * 128:(t + 1) * 128],
                        w2buf[:, par, og * 512:(og + 1) * 512],
                        start=(icl == 0), stop=(icl == NIC - 1))
                    if t == 1 and og == 3:
                        pe.inc(mm)
            nxt = (icl + 2) % NIC
            qw.wait(pe)
            qw.wait(qw)
            qw.inc(qw.e.dma_start(
                out=w2buf[:, par, :],
                in_=w2T_ap[:, nxt, :]))
        dv.wait(pe)
        dv.wait(qgb)
        if upto == "full" and mode == 'l1':
         for t in range(2):
            sch = ccol[:, 9 + 4 * t:10 + 4 * t]
            for og in range(4):
                dv.inc(nc.vector.scalar_tensor_tensor(
                    out=rres[:, t, og * 512:(og + 1) * 512],
                    in0=ps8[:, t * 4 + og, :], scalar=sch,
                    in1=rgrp[:, t, og * 512:(og + 1) * 512],
                    op0=ALU.mult, op1=ALU.add))
                dv.chain()
        if upto == "full" and mode == 'l0':
         for t in range(2):
            for og in range(4):
                dv.inc(nc.vector.tensor_copy(
                    out=rres[:, t, og * 512:(og + 1) * 512],
                    in0=ps8[:, t * 4 + og, :]) if False else
                    nc.vector.tensor_copy(
                        out=rres[:, t, og * 512:(og + 1) * 512],
                        in_=ps8[:, t * 4 + og, :]))
                dv.chain()
            rq = ccol[:, 9 + 4 * t:10 + 4 * t]
            ac.wait(dv)
            ac.inc(nc.scalar.activation(out=tmA[:, :], in_=rres[:, t, :],
                                        func=AF.Square))
            ac.chain()
            dv.wait(ac)
            dv.inc(nc.vector.reduce_sum(out=rq, in_=tmA[:, :], axis=AX))
            dv.chain()
            dv.inc(nc.vector.tensor_scalar(out=rq, in0=rq, scalar1=1.0 / D,
                                           scalar2=1e-6, op0=ALU.mult,
                                           op1=ALU.add))
            dv.chain()
            ac.wait(dv)
            ac.inc(nc.scalar.activation(out=rq, in_=rq, func=AF.Sqrt))
            ac.chain()
            dv.wait(ac)
            dv.inc(nc.vector.reciprocal(out=rq, in_=rq))
            dv.chain()
            dv.inc(nc.vector.scalar_tensor_tensor(
                out=rres[:, t, :], in0=rres[:, t, :], scalar=rq,
                in1=rgrp[:, t, :], op0=ALU.mult, op1=ALU.add))
            dv.chain()
        for t in range(2):
            qg.wait(dv)
            qg.wait(qg)
            qg.inc(qg.e.dma_start(
                out=rgrp[:, t, :],
                in_=bass.AP(Rin_ap.tensor, rctr[t].use(),
                            [[D, 128], [1, D]])))
            rctr[t].adv()
        qgb.wait(dv)
        qgb.inc(qgb.e.dma_start(
            out=bass.AP(Rout_ap.tensor, octr.use(),
                        [[D, 128], [128 * D, 2], [1, D]]),
            in_=rres[:, :, :]))
        octr.adv()
    pb.bar()




def build_l1ffn(registry=None, parts=None, upto='full'):
    pb = PB(registry)
    nc = pb.nc
    Rin = nc.dram_tensor("Rin", [NT, D], F32, kind="ExternalInput").ap()
    w1 = nc.dram_tensor("w_gate_up", [2 * DI, D], F32, kind="ExternalInput").ap()
    w2 = nc.dram_tensor("w_down", [D, DI], F32, kind="ExternalInput").ap()
    Rout = nc.dram_tensor("Rout", [NT, D], F32, kind="ExternalOutput").ap()
    w1T = nc.dram_tensor("w1T", [128, NDC, 2 * DI], BF16, kind="Internal").ap()
    w2T = nc.dram_tensor("w2T", [128, NIC, D], BF16, kind="Internal").ap()

    setup_common(pb)
    if parts is None or "w1" in parts:
        emit_prep_rowquant(pb, w1, w1T, 2 * DI, D, gu_pair_groups(), key="p_w1")
    if parts is None or "w2" in parts:
        emit_prep_rowquant(pb, w2, w2T, D, DI, plain_groups(D), key="p_w2")
    if parts is None or "main" in parts:
        emit_l1ffn_main(pb, Rin, Rout, w1T, w2T, upto=upto)
    else:
        s = pb.qg
        s.inc(pb.nc.gpsimd.dma_start(out=Rout[0:128, :], in_=Rin[0:128, :]))
        pb.bar()
    pb.es.close()
    return pb


def build_twopass(builder, *args, **kwargs):
    pb1 = builder(*args, registry=None, **kwargs)
    reg = pb1.new_registry
    pb2 = builder(*args, registry=reg, **kwargs)
    return pb2


def devrun_l1ffn(parts, seed=7, upto='full'):
    import time
    np.random.seed(seed)
    x = np.random.randn(NT, D).astype(np.float32)
    w1 = (np.random.randn(2 * DI, D) * D ** -0.5).astype(np.float32)
    w2 = (np.random.randn(D, DI) * DI ** -0.5).astype(np.float32)
    pb = build_twopass(build_l1ffn, parts=parts, upto=upto)
    t0 = time.time()
    try:
        res = run_bass_kernel_spmd(pb.nc, [dict(Rin=x, w_gate_up=w1,
                                                w_down=w2)], core_ids=[0])
        print(f"parts={parts} upto={upto}: OK after {time.time()-t0:.0f}s")
        return res
    except Exception as e:
        print(f"parts={parts} upto={upto}: FAILED after {time.time()-t0:.0f}s "
              f"{type(e).__name__}")
        return None




def emit_prep_fp4(pb, wp_ap, sc_ap, wT_ap, O, K, groups, key="pf4"):
    nc = pb.nc
    dv, ac, pe = pb.dv, pb.ac, pb.pe
    qs, qsb, qg = pb.qs, pb.qsb, pb.qg
    KC = K // 128
    KB = K // 2
    w3 = wp_ap.rearrange("(n p) k -> n p k", p=128)
    C = MAGIC
    LN2L = math.log(2.0)

    with contextlib.ExitStack() as es2:
        pbuf = es2.enter_context(nc.sbuf_tensor(key + "_pb", [128, 2, KB], U8))
        sbuf = es2.enter_context(nc.sbuf_tensor(key + "_sc", [KC, 2, 128], F16))
        scf = es2.enter_context(nc.sbuf_tensor(key + "_sf", [KC, 128], F32))
        scT = es2.enter_context(nc.sbuf_tensor(key + "_st", [128, KC], F32))
        bts = es2.enter_context(nc.sbuf_tensor(key + "_bt", [128, KB], F32))
        hi = es2.enter_context(nc.sbuf_tensor(key + "_hi", [128, KB], F32))
        lo = es2.enter_context(nc.sbuf_tensor(key + "_lo", [128, KB], F32))
        t0 = es2.enter_context(nc.sbuf_tensor(key + "_t0", [128, KB], F32))
        t1 = es2.enter_context(nc.sbuf_tensor(key + "_t1", [128, KB], F32))
        t2 = es2.enter_context(nc.sbuf_tensor(key + "_t2", [128, KB], F32))
        dq = es2.enter_context(nc.sbuf_tensor(key + "_dq", [128, K], F32))
        tq = es2.enter_context(nc.sbuf_tensor(key + "_tq", [128, KC, 128], BF16))
        bank = pb.ps8[:, 0, :]

        gsz = len(groups[0])
        ngrp = len(groups)
        base = [groups[0][j] for j in range(gsz)]
        step = ([(groups[1][j][0] - groups[0][j][0],
                  groups[1][j][1] - groups[0][j][1]) for j in range(gsz)]
                if ngrp > 1 else [(0, 0)] * gsz)

        def decode_plane(nib_ap, out_ap):
            ac.wait(dv)
            ac.inc(nc.scalar.activation(out=t0[:, :], in_=nib_ap, func=AF.Sign,
                                        bias=-7.5))
            ac.chain()
            dv.wait(ac)
            dv.inc(nc.vector.scalar_tensor_tensor(
                out=t1[:, :], in0=t0[:, :], scalar=4.0, in1=nib_ap,
                op0=ALU.mult, op1=ALU.subtract))
            dv.chain()
            dv.inc(nc.vector.tensor_scalar(out=t1[:, :], in0=t1[:, :],
                                           scalar1=-1.0, scalar2=-4.0,
                                           op0=ALU.mult, op1=ALU.add))
            dv.chain()
            dv.inc(nc.vector.tensor_scalar(out=t2[:, :], in0=t1[:, :],
                                           scalar1=-0.5, scalar2=0.5,
                                           op0=ALU.add, op1=ALU.mult))
            dv.chain()
            dv.inc(nc.vector.tensor_scalar(out=t2[:, :], in0=t2[:, :],
                                           scalar1=C, scalar2=C,
                                           op0=ALU.add, op1=ALU.subtract))
            dv.chain()
            dv.inc(nc.vector.scalar_tensor_tensor(
                out=t1[:, :], in0=t2[:, :], scalar=-2.0, in1=t1[:, :],
                op0=ALU.mult, op1=ALU.add))
            dv.chain()
            ac.wait(dv)
            ac.inc(nc.scalar.activation(out=nib_ap, in_=t2[:, :], func=AF.Exp,
                                        scale=LN2L, bias=-LN2L))
            ac.chain()
            dv.wait(ac)
            dv.inc(nc.vector.tensor_scalar(out=t2[:, :], in0=t2[:, :],
                                           scalar1=1.0, scalar2=-1.0,
                                           op0=ALU.min, op1=ALU.mult))
            dv.chain()
            dv.inc(nc.vector.tensor_tensor(out=t2[:, :], in0=t2[:, :],
                                           in1=nib_ap, op=ALU.mult))
            dv.chain()
            dv.inc(nc.vector.tensor_scalar(out=nib_ap, in0=nib_ap,
                                           scalar1=1.0, scalar2=None,
                                           op0=ALU.max))
            dv.chain()
            dv.inc(nc.vector.tensor_scalar(out=t1[:, :], in0=t1[:, :],
                                           scalar1=-0.5, scalar2=None,
                                           op0=ALU.mult))
            dv.chain()
            dv.inc(nc.vector.tensor_tensor(out=t1[:, :], in0=t1[:, :],
                                           in1=nib_ap, op=ALU.mult))
            dv.chain()
            dv.inc(nc.vector.tensor_tensor(out=t1[:, :], in0=t1[:, :],
                                           in1=t2[:, :], op=ALU.add))
            dv.chain()
            dv.inc(nc.vector.tensor_tensor(out=out_ap, in0=t1[:, :],
                                           in1=t0[:, :], op=ALU.mult))
            dv.chain()

        for j in range(min(gsz, 2)):
            q = qs if j % 2 == 0 else qsb
            q.inc(q.e.dma_start(out=pbuf[:, j % 2, :],
                                in_=w3[base[j][0], :, :]))
            q.inc(q.e.dma_start(
                out=sbuf[:, j % 2, :],
                in_=sc_ap[:, base[j][0] * 128:(base[j][0] + 1) * 128]))

        TSZ = 128 * KB
        cmax = (O // 128 - 1) * TSZ
        SMX = O - 128
        ctr = []
        sctr = []
        for j in range(gsz):
            nxt = j + 2
            q = qs if j % 2 == 0 else qsb
            sl = nxt - gsz if nxt >= gsz else None
            if nxt < gsz:
                t0j, stj = base[nxt][0], step[nxt][0]
            else:
                t0j, stj = base[sl][0] + step[sl][0], step[sl][0]
            ctr.append(pb.counter(q, f"ctr{j}", t0j * TSZ, stj * TSZ,
                                  cap=cmax))
            sctr.append(pb.counter(q, f"str{j}", t0j * 128, stj * 128,
                                   cap=SMX))
        dctr = [pb.counter(pb.qg, f"dtr{j}", base[j][1], step[j][1],
                           cap=base[j][1] + step[j][1] * max(ngrp - 1, 1))
                for j in range(gsz)]

        with pb.loop(ngrp, key=key) as g:
            for j in range(gsz):
                par = j % 2
                q = qs if par == 0 else qsb
                dv.wait(q)
                dv.wait(qg)
                dv.wait(pe)
                dv.inc(nc.vector.tensor_copy(out=scf[:, :],
                                             in_=sbuf[:, par, :]))
                dv.chain()
                pe.wait(dv)
                pe.inc(nc.tensor.transpose(bank[:, 0:KC], scf[:, :],
                                           pb.ident32[0:KC, 0:KC]))
                dv.wait(pe)
                dv.inc(nc.vector.tensor_copy(out=scT[:, :], in_=bank[:, 0:KC]))
                dv.chain()
                dv.inc(nc.vector.tensor_copy(out=bts[:, :],
                                             in_=pbuf[:, par, :]))
                dv.chain()
                dv.inc(nc.vector.tensor_scalar(
                    out=hi[:, :], in0=bts[:, :], scalar1=-7.5,
                    scalar2=1.0 / 16.0, op0=ALU.add, op1=ALU.mult))
                dv.chain()
                dv.inc(nc.vector.tensor_scalar(out=hi[:, :], in0=hi[:, :],
                                               scalar1=C, scalar2=C,
                                               op0=ALU.add, op1=ALU.subtract))
                dv.chain()
                dv.inc(nc.vector.scalar_tensor_tensor(
                    out=lo[:, :], in0=hi[:, :], scalar=-16.0, in1=bts[:, :],
                    op0=ALU.mult, op1=ALU.add))
                dv.chain()
                q.wait(dv)
                q.wait(q)
                q.inc(q.e.dma_start(
                    out=pbuf[:, par, :],
                    in_=bass.AP(wp_ap.tensor, ctr[j].use(),
                                [[KB, 128], [1, KB]])))
                q.inc(q.e.dma_start(
                    out=sbuf[:, par, :],
                    in_=bass.AP(sc_ap.tensor, sctr[j].use(),
                                [[O, KC], [1, 128]])))
                ctr[j].adv()
                sctr[j].adv()
                dq2 = dq.ap().rearrange("p (a b) -> p a b", b=2)
                decode_plane(lo[:, :], dq2[:, :, 0:1].rearrange("p a b -> p (a b)"))
                decode_plane(hi[:, :], dq2[:, :, 1:2].rearrange("p a b -> p (a b)"))
                for c in range(KC):
                    dv.inc(nc.vector.tensor_scalar(
                        out=dq[:, c * 128:(c + 1) * 128],
                        in0=dq[:, c * 128:(c + 1) * 128],
                        scalar1=scT[:, c:c + 1], scalar2=None, op0=ALU.mult))
                    dv.chain()
                transpose_f32_to_bf16(
                    pb, lambda c: dq[:, c * 128:(c + 1) * 128],
                    lambda c0, w: tq[:, c0:c0 + w, :],
                    KC, bank)
                qg.wait(dv)
                qg.inc(qg.e.dma_start(
                    out=wT_ap[:, :, DS(dctr[j].use(), 128)],
                    in_=tq[:, :, :]))
                dctr[j].adv()
        pb.bar()


def build_l0ffn(registry=None, parts=None, upto='full'):
    pb = PB(registry)
    nc = pb.nc
    Rin = nc.dram_tensor("Rin", [NT, D], F32, kind="ExternalInput").ap()
    gup = nc.dram_tensor("gu_packed_u8", [2 * DI, D // 2], U8,
                         kind="ExternalInput").ap()
    gus = nc.dram_tensor("gu_scales", [D // 128, 2 * DI], F16,
                         kind="ExternalInput").ap()
    dnp = nc.dram_tensor("dn_packed_u8", [D, DI // 2], U8,
                         kind="ExternalInput").ap()
    dns = nc.dram_tensor("dn_scales", [DI // 128, D], F16,
                         kind="ExternalInput").ap()
    Rout = nc.dram_tensor("Rout", [NT, D], F32, kind="ExternalOutput").ap()
    w1T = nc.dram_tensor("w1T", [128, NDC, 2 * DI], BF16, kind="Internal").ap()
    w2T = nc.dram_tensor("w2T", [128, NIC, D], BF16, kind="Internal").ap()

    setup_common(pb)
    if parts is None or "w1" in parts:
        emit_prep_fp4(pb, gup, gus, w1T, 2 * DI, D, gu_pair_groups(),
                      key="pf4a")
    if parts is None or "w2" in parts:
        emit_prep_fp4(pb, dnp, dns, w2T, D, DI, plain_groups(D), key="pf4b")
    if parts is None or "main" in parts:
        emit_l1ffn_main(pb, Rin, Rout, w1T, w2T, upto=upto, mode='l0')
    else:
        s = pb.qg
        s.inc(pb.nc.gpsimd.dma_start(out=Rout[0:128, :], in_=Rin[0:128, :]))
        pb.bar()
    pb.es.close()
    return pb




def emit_prep_ternary(pb, w_ap, wT_ap, gwb_col, onescol, key="pt"):
    nc = pb.nc
    dv, pe = pb.dv, pb.pe
    qs, qsb, qg = pb.qs, pb.qsb, pb.qg
    O = K = D
    KC = NDC
    w3 = w_ap.rearrange("(n p) k -> n p k", p=128)

    with contextlib.ExitStack() as es2:
        pbuf = es2.enter_context(nc.sbuf_tensor(key + "_pb", [128, 2, K], F32))
        wq16 = es2.enter_context(nc.sbuf_tensor(key + "_wq", [128, K], F32))
        tq = es2.enter_context(nc.sbuf_tensor(key + "_tq", [128, KC, 128], BF16))
        acc = es2.enter_context(nc.sbuf_tensor(key + "_ac", [128, 2], F32))
        sca = es2.enter_context(nc.sbuf_tensor(key + "_sc", [1, 2], F32))
        rqc = es2.enter_context(nc.sbuf_tensor(key + "_rq", [128, 1], F32))
        bank = pb.ps8[:, 0, :]
        gp = pb.gp

        gp.inc(nc.gpsimd.memset(acc[:, :], 0.0))
        qs.inc(qs.e.dma_start(out=pbuf[:, 0, :], in_=w3[0, :, :]))
        qsb.inc(qsb.e.dma_start(out=pbuf[:, 1, :], in_=w3[1, :, :]))
        TSZ = 128 * K
        ctrA = [pb.counter(qs, "ctr0", 2 * TSZ, 2 * TSZ,
                           cap=(O // 128 - 1) * TSZ),
                pb.counter(qsb, "ctr1", 3 * TSZ, 2 * TSZ,
                           cap=(O // 128 - 1) * TSZ)]
        dv.wait(gp)
        with pb.loop(O // 256, key=key + "A") as g:
            for par in range(2):
                q = qs if par == 0 else qsb
                dv.wait(q)
                dv.inc(nc.vector.tensor_reduce(
                    out=acc[:, 1:2], in_=pbuf[:, par, :], axis=AX,
                    op=ALU.add, apply_absolute_value=True))
                dv.chain()
                q.wait(dv)
                q.wait(q)
                q.inc(q.e.dma_start(
                    out=pbuf[:, par, :],
                    in_=bass.AP(w_ap.tensor, ctrA[par].use(),
                                [[K, 128], [1, K]])))
                ctrA[par].adv()
                dv.inc(nc.vector.tensor_tensor(out=acc[:, 0:1], in0=acc[:, 0:1],
                                               in1=acc[:, 1:2], op=ALU.add))
                dv.chain()
        pe.wait(dv)
        pe.inc(nc.tensor.matmul(bank[0:1, 0:1], acc[:, 0:1], onescol[:, 0:1],
                                start=True, stop=True))
        dv.wait(pe)
        dv.inc(nc.vector.tensor_scalar(out=sca[:, 0:1], in0=bank[0:1, 0:1],
                                       scalar1=1.0 / (2048.0 * 2048.0),
                                       scalar2=1e-5, op0=ALU.mult, op1=ALU.max))
        dv.chain()
        dv.inc(nc.vector.reciprocal(out=sca[:, 1:2], in_=sca[:, 0:1]))
        dv.chain()
        pe.wait(dv)
        pe.inc(nc.tensor.matmul(bank[:, 0:2], pb.onesrow[0:1, :],
                                sca[0:1, 0:2], start=True, stop=True))
        dv.wait(pe)
        dv.inc(nc.vector.tensor_copy(out=gwb_col, in_=bank[:, 0:1]))
        dv.chain()
        dv.inc(nc.vector.tensor_copy(out=rqc[:, :], in_=bank[:, 1:2]))
        dv.chain()

        qs.wait(qs)
        qsb.wait(qsb)
        qs.inc(qs.e.dma_start(out=pbuf[:, 0, :], in_=w3[0, :, :]))
        qsb.inc(qsb.e.dma_start(out=pbuf[:, 1, :], in_=w3[1, :, :]))
        ctrB = [pb.counter(qs, "ctr0", 2 * TSZ, 2 * TSZ,
                           cap=(O // 128 - 1) * TSZ),
                pb.counter(qsb, "ctr1", 3 * TSZ, 2 * TSZ,
                           cap=(O // 128 - 1) * TSZ)]
        dctr2 = [pb.counter(qg, "dta0", 0, 256, cap=(O // 128 - 2) * 128),
                 pb.counter(qg, "dta1", 128, 256, cap=(O // 128 - 1) * 128)]
        with pb.loop(O // 256, key=key + "B") as g:
            for par in range(2):
                q = qs if par == 0 else qsb
                dv.wait(q)
                dv.wait(qg)
                dv.wait(pe)
                dv.inc(nc.vector.tensor_scalar(
                    out=wq16[:, :], in0=pbuf[:, par, :], scalar1=rqc[:, 0:1],
                    scalar2=MAGIC, op0=ALU.mult, op1=ALU.add))
                dv.chain()
                dv.inc(nc.vector.tensor_scalar(
                    out=wq16[:, :], in0=wq16[:, :], scalar1=MAGIC,
                    scalar2=-1.0, op0=ALU.subtract, op1=ALU.max))
                dv.chain()
                dv.inc(nc.vector.tensor_scalar(
                    out=wq16[:, :], in0=wq16[:, :], scalar1=1.0, scalar2=None,
                    op0=ALU.min))
                dv.chain()
                q.wait(dv)
                q.wait(q)
                q.inc(q.e.dma_start(
                    out=pbuf[:, par, :],
                    in_=bass.AP(w_ap.tensor, ctrB[par].use(),
                                [[K, 128], [1, K]])))
                ctrB[par].adv()
                transpose_f32_to_bf16(
                    pb, lambda c: wq16[:, c * 128:(c + 1) * 128],
                    lambda c0, w: tq[:, c0:c0 + w, :],
                    KC, bank)
                qg.wait(dv)
                qg.inc(qg.e.dma_start(
                    out=wT_ap[:, :, DS(dctr2[par].use(), 128)],
                    in_=tq[:, :, :]))
                dctr2[par].adv()
        pb.bar()




def emit_l1attn(pb, Rin_ap, Rout_ap, wTs, gwb, nwbc, gnw_col, hmid_ap,
                hend_ap, prefix_only=False):
    nc = pb.nc
    dv, ac, pe, gp = pb.dv, pb.ac, pb.pe, pb.gp
    qs, qsb, qg, qgb = pb.qs, pb.qsb, pb.qg, pb.qgb
    NG = 16
    GT = 256
    ps8 = pb.ps8
    IB = {"i": 0, "f": 1, "g": 2, "o": 3}

    rt = pb.sb("a1_rt", [128, D], F32)
    z = pb.sb("a1_z", [128, 2, D], F32)
    tmB = pb.sb("a1_tmB", [128, D], F32)
    xqf = pb.sb("a1_xqf", [128, NDC, GT], BF16)
    wbuf = pb.sb("a1_wbuf", [128, 2, NDC, 512], BF16)
    fFM = pb.sb("a1_f", [128, NDC, GT], F32)
    iaFM = pb.sb("a1_ia", [128, NDC, GT], F32)
    hFM = pb.sb("a1_h", [128, NDC, GT], F32)
    gFM = pb.sb("a1_g", [128, NDC, GT], F32)
    carry = pb.sb("a1_carry", [128, NDC], F32)
    cols = pb.sb("a1_cols", [128, 16], F32)
    rows = pb.sb("a1_rows", [1, 8 * GT], F32)
    osbc = pb.sb("a1_osbc", [128, 3, GT], F32)
    e256 = pb.sb("a1_e256", [128, GT], F32)

    Rin3 = Rin_ap.rearrange("(a p) d -> p a d", p=128)
    Rout3 = Rout_ap.rearrange("(a p) d -> p a d", p=128)

    qg.inc(qg.e.dma_start(out=carry[:, :], in_=hmid_ap[:, :]))
    RT = 128 * D
    rctr = [pb.counter(qg, "ra0", 0, 2 * RT, cap=(2 * NG - 2) * RT),
            pb.counter(qg, "ra1", RT, 2 * RT, cap=(2 * NG - 1) * RT)]
    rctr2 = [pb.counter(qg, "rb0", 0, 2 * RT, cap=(2 * NG - 2) * RT),
             pb.counter(qg, "rb1", RT, 2 * RT, cap=(2 * NG - 1) * RT)]
    octr = [pb.counter(qgb, "oa0", 0, 2 * RT, cap=(2 * NG - 2) * RT),
            pb.counter(qgb, "oa1", RT, 2 * RT, cap=(2 * NG - 1) * RT)]
    worder = ["f", "i"] if prefix_only else ["f", "i", "g", "o"]
    NSP = len(worder) * 4
    wctr = None

    def span_src(pos):
        b = worder[pos // 4]
        sp = pos % 4
        return wTs[b][:, :, sp * 512:(sp + 1) * 512]

    qs.inc(qs.e.dma_start(out=wbuf[:, 0, :, :], in_=span_src(0)))
    qsb.inc(qsb.e.dma_start(out=wbuf[:, 1, :, :], in_=span_src(1)))

    with pb.loop(NG, key="a1_main" + ("P" if prefix_only else "")) as g:
        dv.wait(pe)
        dv.wait(qgb)
        for t in range(2):
            qg.wait(qg)
            qg.wait(dv)
            qg.inc(qg.e.dma_start(
                out=rt[:, :],
                in_=bass.AP(Rin_ap.tensor, rctr[t].use(), [[D, 128], [1, D]])))
            rctr[t].adv()
            dv.wait(qg)
            ln_tile(pb, rt[:, :], z[:, t, :], tmB[:, :], cols, 0)
        for b in (["f", "i"] if prefix_only else ["f", "i", "g"]):
            for t in range(2):
                dv.inc(nc.vector.tensor_tensor(
                    out=tmB[:, :], in0=z[:, t, :], in1=nwbc[b][:, :],
                    op=ALU.mult))
                dv.chain()
                am = cols[:, 4:5]
                c1 = cols[:, 5:6]
                rc = cols[:, 6:7]
                sc = cols[:, 7:8]
                dv.inc(nc.vector.reduce_max(out=am, in_=tmB[:, :], axis=AX,
                                            apply_absolute_value=True))
                dv.chain()
                dv.inc(nc.vector.tensor_scalar(out=c1, in0=am, scalar1=1e-5,
                                               scalar2=None, op0=ALU.max))
                dv.chain()
                dv.inc(nc.vector.reciprocal(out=rc, in_=c1))
                dv.chain()
                dv.inc(nc.vector.tensor_scalar(out=sc, in0=rc, scalar1=127.0,
                                               scalar2=None, op0=ALU.mult))
                dv.chain()
                dv.inc(nc.vector.scalar_tensor_tensor(
                    out=cols[:, 8 + t:9 + t], in0=c1, scalar=1.0 / 127.0,
                    in1=gwb[:, IB[b]:IB[b] + 1], op0=ALU.mult, op1=ALU.mult))
                dv.chain()
                dv.inc(nc.vector.tensor_scalar(out=tmB[:, :], in0=tmB[:, :],
                                               scalar1=sc, scalar2=MAGIC,
                                               op0=ALU.mult, op1=ALU.add))
                dv.chain()
                dv.inc(nc.vector.tensor_scalar(out=tmB[:, :], in0=tmB[:, :],
                                               scalar1=MAGIC, scalar2=None,
                                               op0=ALU.subtract))
                dv.chain()
                transpose_f32_to_bf16(
                    pb, lambda c: tmB[:, c * 128:(c + 1) * 128],
                    lambda c0, w: xqf.ap()[0:128, c0:c0 + w,
                                           t * 128:(t + 1) * 128],
                    NDC, ps8[:, 4, :])
                pe.wait(dv)
                pe.inc(nc.tensor.transpose(ps8[0:1, 5, 0:128],
                                           cols[:, 8 + t:9 + t],
                                           pb.ident32[:]))
                dv.wait(pe)
                dv.inc(nc.vector.tensor_copy(
                    out=rows[:, t * 128:(t + 1) * 128],
                    in_=ps8[0:1, 5, 0:128]))
                dv.chain()
            pe.wait(dv)
            pe.inc(nc.tensor.matmul(ps8[:, 5, 0:GT], pb.onesrow[0:1, :],
                                    rows[0:1, 0:GT], start=True, stop=True))
            dv.wait(pe)
            dv.inc(nc.vector.tensor_copy(out=osbc[:, IB[b], :],
                                         in_=ps8[:, 5, 0:GT]))
            dv.chain()
            wpos0 = worder.index(b) * 4
            for sp in range(4):
                pos = wpos0 + sp
                par = pos % 2
                qw = qs if par == 0 else qsb
                pe.wait(qw)
                pe.wait(dv)
                pe.wait(ac)
                for dc in range(NDC):
                    for oc in range(4):
                        mm = nc.tensor.matmul(
                            ps8[:, oc, 0:GT],
                            wbuf[:, par, dc, oc * 128:(oc + 1) * 128],
                            xqf[:, dc, :],
                            start=(dc == 0), stop=(dc == NDC - 1))
                        if dc == NDC - 1 and oc == 3:
                            pe.inc(mm)
                qw.wait(pe)
                qw.wait(qw)
                nxt = (pos + 2) % NSP
                qw.inc(qw.e.dma_start(out=wbuf[:, par, :, :],
                                      in_=span_src(nxt)))
                oco = sp * 4
                for oc in range(4):
                    src = ps8[:, oc, 0:GT]
                    dst_c = oco + oc
                    dv.wait(pe)
                    dv.wait(ac)
                    if b == "f":
                        dv.inc(nc.vector.tensor_tensor(
                            out=e256[:, :], in0=src, in1=osbc[:, 1, :],
                            op=ALU.mult))
                        dv.chain()
                        ac.wait(dv)
                        ac.inc(nc.scalar.activation(
                            out=fFM[:, dst_c, :], in_=e256[:, :],
                            func=AF.Sigmoid))
                        ac.chain()
                    elif b == "i":
                        dv.inc(nc.vector.tensor_tensor(
                            out=e256[:, :], in0=src, in1=osbc[:, 0, :],
                            op=ALU.mult))
                        dv.chain()
                        ac.wait(dv)
                        ac.inc(nc.scalar.activation(
                            out=e256[:, :], in_=e256[:, :], func=AF.Silu))
                        ac.chain()
                        dv.wait(ac)
                        dv.inc(nc.vector.tensor_scalar(
                            out=iaFM[:, dst_c, :], in0=fFM[:, dst_c, :],
                            scalar1=-1.0, scalar2=1.0, op0=ALU.mult,
                            op1=ALU.add))
                        dv.chain()
                        dv.inc(nc.vector.tensor_tensor(
                            out=iaFM[:, dst_c, :], in0=iaFM[:, dst_c, :],
                            in1=e256[:, :], op=ALU.mult))
                        dv.chain()
                    else:
                        dv.inc(nc.vector.tensor_tensor(
                            out=gFM[:, dst_c, :], in0=src, in1=osbc[:, 2, :],
                            op=ALU.mult))
                        dv.chain()
        for dc in range(NDC):
            dv.inc(nc.vector.tensor_tensor_scan(
                out=hFM[:, dc, :], data0=fFM[:, dc, :], data1=iaFM[:, dc, :],
                initial=carry[:, dc:dc + 1], op0=ALU.mult, op1=ALU.add))
            dv.chain()
        dv.inc(nc.vector.tensor_copy(
            out=carry[:, :],
            in_=hFM.ap()[0:128, :, GT - 1:GT].rearrange("p a b -> p (a b)")))
        dv.chain()
        if not prefix_only:
            ac.wait(dv)
            ac.inc(nc.scalar.activation(
                out=fFM.ap()[0:128, :, :].rearrange("p a b -> p (a b)"),
                in_=gFM.ap()[0:128, :, :].rearrange("p a b -> p (a b)"),
                func=AF.Square))
            ac.chain()
            pe.wait(ac)
            for dc in range(NDC):
                mm = nc.tensor.matmul(ps8[0:1, 5, 0:GT], pb.onescol32[:, 0:1],
                                      fFM[:, dc, :],
                                      start=(dc == 0), stop=(dc == NDC - 1))
                if dc == NDC - 1:
                    pe.inc(mm)
            dv.wait(pe)
            r0 = rows[:, 0:GT]
            dv.inc(nc.vector.tensor_scalar(out=r0, in0=ps8[0:1, 5, 0:GT],
                                           scalar1=1.0 / D, scalar2=1e-5,
                                           op0=ALU.mult, op1=ALU.add))
            dv.chain()
            ac.wait(dv)
            ac.inc(nc.scalar.activation(out=r0, in_=r0, func=AF.Sqrt))
            ac.chain()
            dv.wait(ac)
            dv.inc(nc.vector.reciprocal(out=r0, in_=r0))
            dv.chain()
            pe.wait(dv)
            pe.inc(nc.tensor.matmul(ps8[:, 5, 0:GT], pb.onesrow[0:1, :],
                                    rows[0:1, 0:GT], start=True, stop=True))
            ac.wait(dv)
            ac.inc(nc.scalar.activation(
                out=iaFM.ap()[0:128, :, :].rearrange("p a b -> p (a b)"),
                in_=hFM.ap()[0:128, :, :].rearrange("p a b -> p (a b)"),
                func=AF.Silu))
            ac.chain()
            dv.wait(pe)
            dv.wait(ac)
            for dc in range(NDC):
                dv.inc(nc.vector.scalar_tensor_tensor(
                    out=gFM[:, dc, :], in0=gFM[:, dc, :],
                    scalar=gnw_col[:, dc:dc + 1], in1=ps8[:, 5, 0:GT],
                    op0=ALU.mult, op1=ALU.mult))
                dv.chain()
                dv.inc(nc.vector.tensor_tensor(
                    out=gFM[:, dc, :], in0=gFM[:, dc, :], in1=iaFM[:, dc, :],
                    op=ALU.mult))
                dv.chain()
            for t in range(2):
                pe.wait(dv)
                for c0 in range(0, NDC, 4):
                    pe.wait(dv)
                    for c in range(c0, c0 + 4):
                        tr = nc.tensor.transpose(
                            ps8[:, 4, (c - c0) * 128:(c - c0 + 1) * 128],
                            gFM[:, c, t * 128:(t + 1) * 128], pb.ident32[:])
                        if c == c0 + 3:
                            pe.inc(tr)
                    dv.wait(pe)
                    dv.inc(nc.vector.tensor_copy(
                        out=z[:, t, c0 * 128:(c0 + 4) * 128],
                        in_=ps8[:, 4, 0:512]))
                    dv.chain()
            for t in range(2):
                am = cols[:, 4:5]
                c1 = cols[:, 5:6]
                rc = cols[:, 6:7]
                sc = cols[:, 7:8]
                ac.wait(dv)
                ac.inc(nc.scalar.activation(out=tmB[:, :], in_=z[:, t, :],
                                            func=AF.Square))
                ac.chain()
                dv.wait(ac)
                dv.inc(nc.vector.reduce_sum(out=am, in_=tmB[:, :], axis=AX))
                dv.chain()
                dv.inc(nc.vector.tensor_scalar(out=am, in0=am, scalar1=1.0 / D,
                                               scalar2=1e-8, op0=ALU.mult,
                                               op1=ALU.add))
                dv.chain()
                ac.wait(dv)
                ac.inc(nc.scalar.activation(out=am, in_=am, func=AF.Sqrt))
                ac.chain()
                dv.wait(ac)
                dv.inc(nc.vector.reciprocal(out=am, in_=am))
                dv.chain()
                dv.inc(nc.vector.tensor_scalar(out=z.ap()[0:128, t, :],
                                               in0=z[:, t, :], scalar1=am,
                                               scalar2=None, op0=ALU.mult))
                dv.chain()
                dv.inc(nc.vector.tensor_tensor(out=tmB[:, :], in0=z[:, t, :],
                                               in1=nwbc["o"][:, :],
                                               op=ALU.mult))
                dv.chain()
                dv.inc(nc.vector.reduce_max(out=am, in_=tmB[:, :], axis=AX,
                                            apply_absolute_value=True))
                dv.chain()
                dv.inc(nc.vector.tensor_scalar(out=c1, in0=am, scalar1=1e-5,
                                               scalar2=None, op0=ALU.max))
                dv.chain()
                dv.inc(nc.vector.reciprocal(out=rc, in_=c1))
                dv.chain()
                dv.inc(nc.vector.tensor_scalar(out=sc, in0=rc, scalar1=127.0,
                                               scalar2=None, op0=ALU.mult))
                dv.chain()
                dv.inc(nc.vector.scalar_tensor_tensor(
                    out=cols[:, 10 + t:11 + t], in0=c1, scalar=1.0 / 127.0,
                    in1=gwb[:, 3:4], op0=ALU.mult, op1=ALU.mult))
                dv.chain()
                dv.inc(nc.vector.tensor_scalar(out=tmB[:, :], in0=tmB[:, :],
                                               scalar1=sc, scalar2=MAGIC,
                                               op0=ALU.mult, op1=ALU.add))
                dv.chain()
                dv.inc(nc.vector.tensor_scalar(out=tmB[:, :], in0=tmB[:, :],
                                               scalar1=MAGIC, scalar2=None,
                                               op0=ALU.subtract))
                dv.chain()
                transpose_f32_to_bf16(
                    pb, lambda c: tmB[:, c * 128:(c + 1) * 128],
                    lambda c0, w: xqf.ap()[0:128, c0:c0 + w,
                                           t * 128:(t + 1) * 128],
                    NDC, ps8[:, 4, :])
            for sp in range(4):
                pos = 12 + sp
                par = pos % 2
                qw = qs if par == 0 else qsb
                pe.wait(qw)
                pe.wait(dv)
                for t in range(2):
                    for dc in range(NDC):
                        mm = nc.tensor.matmul(
                            ps8[:, t * 4 + sp, :],
                            xqf[:, dc, t * 128:(t + 1) * 128],
                            wbuf[:, par, dc, :],
                            start=(dc == 0), stop=(dc == NDC - 1))
                        if dc == NDC - 1:
                            pe.inc(mm)
                qw.wait(pe)
                qw.wait(qw)
                nxt = (pos + 2) % NSP
                qw.inc(qw.e.dma_start(out=wbuf[:, par, :, :],
                                      in_=span_src(nxt)))
            for t in range(2):
                qg.wait(qg)
                qg.wait(dv)
                qg.inc(qg.e.dma_start(
                    out=rt[:, :],
                    in_=bass.AP(Rin_ap.tensor, rctr2[t].use(),
                                [[D, 128], [1, D]])))
                rctr2[t].adv()
                dv.wait(qg)
                dv.wait(pe)
                for sp in range(4):
                    dv.inc(nc.vector.scalar_tensor_tensor(
                        out=z.ap()[0:128, t, sp * 512:(sp + 1) * 512],
                        in0=ps8[:, t * 4 + sp, :],
                        scalar=cols[:, 10 + t:11 + t],
                        in1=rt[:, sp * 512:(sp + 1) * 512],
                        op0=ALU.mult, op1=ALU.add))
                    dv.chain()
                qgb.wait(dv)
                qgb.inc(qgb.e.dma_start(
                    out=bass.AP(Rout_ap.tensor, octr[t].use(),
                                [[D, 128], [1, D]]),
                    in_=z[:, t, :]))
                octr[t].adv()
    qg.wait(dv)
    qg.inc(qg.e.dma_start(out=hend_ap[:, :], in_=carry[:, :]))
    pb.bar()


def setup_l1attn_consts(pb, norm_aps):
    nc = pb.nc
    dv, pe, gp, qg = pb.dv, pb.pe, pb.gp, pb.qg
    nwbc = {}
    rowbuf = pb.sb("a1c_row", [1, D], F32)
    g16 = pb.sb("a1c_g16", [16, 128], F32)
    gnw_col = pb.sb("a1c_gnw", [128, NDC], F32)
    for b in ("i", "f", "g", "o"):
        nwbc[b] = pb.sb(f"a1c_nw{b}", [128, D], F32)
    for b in ("i", "f", "g", "o"):
        src = norm_aps[b]
        qg.wait(dv)
        qg.inc(qg.e.dma_start(out=rowbuf[:, :], in_=src.rearrange("(a d) -> a d", a=1)))
        dv.wait(qg)
        pe.wait(qg)
        for k in range(4):
            pe.wait(dv)
            pe.inc(nc.tensor.matmul(pb.ps8[:, 6, 0:512], pb.onesrow[0:1, :],
                                    rowbuf[0:1, k * 512:(k + 1) * 512],
                                    start=True, stop=True))
            dv.wait(pe)
            dv.inc(nc.vector.tensor_copy(out=nwbc[b][:, k * 512:(k + 1) * 512],
                                         in_=pb.ps8[:, 6, 0:512]))
            dv.chain()
    qg.wait(dv)
    qg.inc(qg.e.dma_start(out=g16[:, :],
                          in_=norm_aps["gn"].rearrange("(a d) -> a d", a=16)))
    dv.wait(qg)
    pe.wait(qg)
    pe.wait(dv)
    pe.inc(nc.tensor.transpose(pb.ps8[:, 6, 0:16], g16[:, :],
                               pb.ident32[0:16, 0:16]))
    dv.wait(pe)
    dv.inc(nc.vector.tensor_copy(out=gnw_col[:, :], in_=pb.ps8[:, 6, 0:16]))
    dv.chain()
    pb.bar()
    return nwbc, gnw_col


def build_l1attn(registry=None, prefix_only=False, parts=None):
    pb = PB(registry)
    nc = pb.nc
    pb.onesrow = pb.sb("g_ones_r", [1, 128], F32)
    pb.onescol32 = pb.sb("g_ones_c", [128, 1], F32)
    Rin = nc.dram_tensor("Rin", [NT, D], F32, kind="ExternalInput").ap()
    Wi = nc.dram_tensor("Wi", [D, D], F32, kind="ExternalInput").ap()
    Wf = nc.dram_tensor("Wf", [D, D], F32, kind="ExternalInput").ap()
    Wg = nc.dram_tensor("Wg", [D, D], F32, kind="ExternalInput").ap()
    Wo = nc.dram_tensor("Wo", [D, D], F32, kind="ExternalInput").ap()
    nmi = nc.dram_tensor("norm_i", [D], F32, kind="ExternalInput").ap()
    nmf = nc.dram_tensor("norm_f", [D], F32, kind="ExternalInput").ap()
    nmg = nc.dram_tensor("norm_g", [D], F32, kind="ExternalInput").ap()
    nmo = nc.dram_tensor("norm_o", [D], F32, kind="ExternalInput").ap()
    gnw = nc.dram_tensor("g_norm_w", [D], F32, kind="ExternalInput").ap()
    hmid = nc.dram_tensor("hmid", [128, NDC], F32, kind="ExternalInput").ap()
    Rout = nc.dram_tensor("Rout", [NT, D], F32, kind="ExternalOutput").ap()
    hend = nc.dram_tensor("hend", [128, NDC], F32, kind="ExternalOutput").ap()
    wTs = {b: nc.dram_tensor(f"wT{b}", [128, NDC, D], BF16,
                             kind="Internal").ap()
           for b in ("i", "f", "g", "o")}
    gwb = pb.sb("a1_gwb", [128, 4], F32)

    setup_common(pb)
    gp = pb.gp
    gp.inc(nc.gpsimd.memset(pb.onesrow[:, :], 1.0))
    gp.inc(nc.gpsimd.memset(pb.onescol32[:, :], 1.0))
    pb.bar()
    nwbc, gnw_col = setup_l1attn_consts(
        pb, dict(i=nmi, f=nmf, g=nmg, o=nmo, gn=gnw))
    wsrc = dict(i=Wi, f=Wf, g=Wg, o=Wo)
    blist = ["i", "f"] if prefix_only else ["i", "f", "g", "o"]
    for b in blist:
        if parts is None or b in parts:
            emit_prep_ternary(pb, wsrc[b], wTs[b], gwb[:, "ifgo".index(b):
                                                       "ifgo".index(b) + 1],
                              pb.onescol32, key=f"pt{b}")
    if parts is None or "main" in parts:
        emit_l1attn(pb, Rin, Rout, wTs, gwb, nwbc, gnw_col, hmid, hend,
                    prefix_only=prefix_only)
    else:
        pb.qg.inc(nc.gpsimd.dma_start(out=Rout[0:128, :], in_=Rin[0:128, :]))
        pb.qg.inc(nc.gpsimd.dma_start(out=hend[:, :], in_=hmid[:, :]))
        pb.bar()
    pb.es.close()
    return pb




def emit_l0attn(pb, Rin_ap, Rout_ap, slope_ap, wqkT_ap, woT_ap, kvmid_ap,
                kvend_ap, qkvu_ap, prefix_only=False):
    nc = pb.nc
    dv, ac, pe, gp = pb.dv, pb.ac, pb.pe, pb.gp
    qs, qsb, qg, qgb = pb.qs, pb.qsb, pb.qg, pb.qgb
    NH, HD, BLK = 16, 128, 256
    NB = NT // BLK
    ps8 = pb.ps8
    QW = 8192

    rrow = pb.sb("a0_rrow", [1, 16], F32)
    rbc = pb.sb("a0_rbc", [128, 16], F32)
    bdc = pb.sb("a0_bd", [128, 16], F32)
    biasc = pb.sb("a0_bias", [128, 4], F32)
    iotac = pb.sb("a0_iota", [128, 1], F32)
    deltp = pb.sb("a0_delt", [128, 2, BLK], F32)
    caus = pb.sb("a0_caus", [128, 2, BLK], F32)
    maskf = pb.sb("a0_mask", [128, 2, BLK], F32)

    qg.inc(qg.e.dma_start(out=rrow[:, :],
                          in_=slope_ap.rearrange("(a h) b c -> a (h b c)", a=1)))
    gp.inc(nc.gpsimd.iota(iotac[:], pattern=[[1, 1]], base=0,
                          channel_multiplier=1,
                          allow_small_or_imprecise_dtypes=True))
    for sc in range(2):
        gp.chain()
        gp.inc(nc.gpsimd.iota(deltp[:, sc, :], pattern=[[1, BLK]],
                              base=-128 * sc, channel_multiplier=-1,
                              allow_small_or_imprecise_dtypes=True))
    gp.chain()
    dv.wait(gp)
    dv.inc(nc.vector.tensor_scalar(out=caus.ap()[0:128, :, :],
                                   in0=deltp.ap()[0:128, :, :], scalar1=0.0,
                                   scalar2=None, op0=ALU.is_ge))
    dv.chain()
    dv.inc(nc.vector.tensor_scalar(out=deltp.ap()[0:128, :, :],
                                   in0=deltp.ap()[0:128, :, :], scalar1=0.0,
                                   scalar2=None, op0=ALU.max))
    dv.chain()
    pe.wait(dv)
    pe.wait(qg)
    pe.inc(nc.tensor.matmul(ps8[:, 6, 0:16], pb.onesrow[0:1, :], rrow[0:1, :],
                            start=True, stop=True))
    dv.wait(pe)
    dv.inc(nc.vector.tensor_copy(out=rbc[:, :], in_=ps8[:, 6, 0:16]))
    dv.chain()
    dv.inc(nc.vector.tensor_scalar(out=bdc[:, :], in0=rbc[:, :],
                                   scalar1=-256.0, scalar2=None, op0=ALU.mult))
    dv.chain()
    ac.wait(dv)
    ac.inc(nc.scalar.activation(out=bdc[:, :], in_=bdc[:, :], func=AF.Exp))
    ac.chain()
    nrbc = pb.sb("a0_nrbc", [128, 16], F32)
    dv.inc(nc.vector.tensor_scalar(out=nrbc[:, :], in0=rbc[:, :],
                                   scalar1=-1.0, scalar2=None, op0=ALU.mult))
    dv.chain()
    for _ in range(6):
        ac.inc(nc.scalar.activation(out=biasc[:, 0:1], in_=iotac[:, :],
                                    func=AF.Copy))
        ac.chain()
    pb.bar()

    rt = pb.sb("a0_rt", [128, D], F32)
    tmA = pb.sb("a0_tmA", [128, D], F32)
    xnf = pb.sb("a0_xnf", [128, NDC, 128], BF16)
    wbuf = pb.sb("a0_wbuf", [128, 2, NDC, 512], BF16)
    stg = pb.sb("a0_stg", [128, 4, 512], F32)
    cols = pb.sb("a0_cols", [128, 16], F32)

    spans = list(range(4, 12)) if prefix_only else list(range(16))
    NSP = len(spans)
    qs.inc(qs.e.dma_start(out=wbuf[:, 0, :, :],
                          in_=wqkT_ap[:, :, spans[0] * 512:spans[0] * 512 + 512]))
    qsb.inc(qsb.e.dma_start(out=wbuf[:, 1, :, :],
                            in_=wqkT_ap[:, :, spans[1] * 512:spans[1] * 512 + 512]))
    lo = spans[0] * 512
    RT = 128 * D
    rctrA = pb.counter(qg, "ra0", 0, RT, cap=(NT // 128 - 1) * RT)
    SWA = NSP * 512
    octrA = [pb.counter(qgb, f"qa{i}", 4 * i * 65536, NSP * 65536,
                        cap=((NT // 128 - 1) * NSP + 4 * i) * 65536)
             for i in range(NSP // 4)]



    with pb.loop(NT // 128, key="a0_A" + ("P" if prefix_only else "")) as ti:
        qg.wait(qg)
        qg.wait(dv)
        qg.wait(ac)
        qg.inc(qg.e.dma_start(
            out=rt[:, :],
            in_=bass.AP(Rin_ap.tensor, rctrA.use(), [[D, 128], [1, D]])))
        rctrA.adv()
        dv.wait(qg)
        dv.wait(pe)
        ln_tile(pb, rt[:, :], tmA[:, :], rt[:, :], cols, 0)
        transpose_f32_to_bf16(
            pb, lambda c: tmA[:, c * 128:(c + 1) * 128],
            lambda c0, w: xnf.ap()[0:128, c0:c0 + w, 0:128],
            NDC, ps8[:, 6, :])
        pe.wait(dv)
        for sp in range(NSP):
            par = sp % 2
            qw = qs if par == 0 else qsb
            pe.wait(qw)
            pe.wait(ac, back=(5 if sp < 4 else 3))
            for dc in range(NDC):
                mm = nc.tensor.matmul(
                    ps8[:, sp % 4, :], xnf[:, dc, 0:128],
                    wbuf[:, par, dc, :],
                    start=(dc == 0), stop=(dc == NDC - 1))
                if dc == NDC - 1:
                    pe.inc(mm)
            qw.wait(pe)
            qw.wait(qw)
            nsp2 = (sp + 2) % NSP
            qw.inc(qw.e.dma_start(
                out=wbuf[:, par, :, :],
                in_=wqkT_ap[:, :, lo + nsp2 * 512:lo + nsp2 * 512 + 512]))
            gspan = spans[sp]
            dst = stg[:, sp % 4, :]
            ac.wait(pe)
            ac.wait(qgb)
            ac.inc(nc.scalar.activation(
                out=dst, in_=ps8[:, sp % 4, :],
                func=(AF.Silu if gspan < 8 else AF.Copy)))
            if sp % 4 == 3:
                qgb.wait(ac)
                qgb.inc(qgb.e.dma_start(
                    out=bass.AP(qkvu_ap.tensor, octrA[sp // 4].use(),
                                [[512, 128], [65536, 4], [1, 512]]),
                    in_=stg[:, :, :]))
                octrA[sp // 4].adv()
    pb.bar()

    SW = NSP * 512
    NSPQ = 12 if not prefix_only else 8
    koff = 0 if not prefix_only else -2048
    qb = pb.sb("a0_qb", [128, 2, NSPQ * 512], F32)
    kvst = pb.sb("a0_kvst", [128, NH, HD], F32)
    kv16 = pb.sb("a0_kv16", [128, HD], BF16)
    kt16 = pb.sb("a0_kt16", [128, 2, D], BF16)
    vt16 = pb.sb("a0_vt16", [128, 2, D], BF16)
    qkf16 = pb.sb("a0_qkf16", [128, 4, HD], BF16)
    sm16 = pb.sb("a0_sm16", [128, 2, BLK], BF16)
    obig = pb.sb("a0_obig", [128, 2, D], F32)
    ouf = pb.sb("a0_ouf", [128, NDC, BLK], BF16)
    kdecbig = pb.sb("a0_kdec", [128, 2, D], F32)
    maskbig = pb.sb("a0_maskb", [128, NH, 2, BLK], BF16)
    qdecc = pb.sb("a0_qdec", [128, 2, NH], F32)
    biasK = pb.sb("a0_biasK", [128, 2, NH], F32)
    iotab = pb.sb("a0_iotab", [128, 128], F32)

    gp.chain()
    gp.inc(nc.gpsimd.iota(iotab[:], pattern=[[0, 128]], base=0,
                          channel_multiplier=1,
                          allow_small_or_imprecise_dtypes=True))
    gp.chain()
    for tc in range(2):
        dv.inc(nc.vector.tensor_scalar(
            out=biasK[:, tc, :], in0=rbc[:, :],
            scalar1=float(128 * tc - 255), scalar2=None, op0=ALU.mult))
        dv.chain()
    ac.wait(dv)
    ac.wait(gp)
    for tc in range(2):
        for h in range(NH):
            ac.inc(nc.scalar.activation(
                out=kdecbig[:, tc, h * HD:(h + 1) * HD], in_=iotab[:, :],
                func=AF.Exp, scale=rbc[:, h:h + 1],
                bias=biasK[:, tc, h:h + 1]))
    ac.chain()
    if not prefix_only:
        dv.wait(ac)
        for tc in range(2):
            dv.inc(nc.vector.tensor_scalar(
                out=biasK[:, tc, :], in0=nrbc[:, :],
                scalar1=float(1 + 128 * tc), scalar2=None, op0=ALU.mult))
            dv.chain()
        ac.wait(dv)
        for tc in range(2):
            for h in range(NH):
                ac.inc(nc.scalar.activation(
                    out=qdecc[:, tc, h:h + 1], in_=iotac[:, :],
                    func=AF.Exp, scale=nrbc[:, h:h + 1],
                    bias=biasK[:, tc, h:h + 1]))
        ac.chain()
        for h in range(NH):
            ac.wait(dv)
            ac.inc(nc.scalar.activation(
                out=maskf.ap()[0:128, :, :], in_=deltp.ap()[0:128, :, :],
                func=AF.Exp, scale=nrbc[:, h:h + 1]))
            ac.chain()
            dv.wait(ac)
            dv.inc(nc.vector.tensor_tensor(
                out=maskbig.ap()[0:128, h, :, :],
                in0=maskf.ap()[0:128, :, :],
                in1=caus.ap()[0:128, :, :], op=ALU.mult))
            dv.chain()
    pb.bar()

    qg.wait(qgb)
    qg.inc(qg.e.dma_start(out=kvst.ap()[0:128, :, :], in_=kvmid_ap[:, :].rearrange(
        "p (h e) -> p h e", h=NH)))
    QB = 128 * SW
    bctr = [pb.counter(qg, f"bb{tc}", tc * QB, 2 * QB,
                       cap=(NT // 128 - 2 + tc) * QB) for tc in range(2)]
    rctrB = [pb.counter(qg, "rb0", 0, 2 * RT, cap=(NT // 128 - 2) * RT),
             pb.counter(qg, "rb1", RT, 2 * RT, cap=(NT // 128 - 1) * RT)]
    octrB = [pb.counter(qgb, "ob0", 0, 2 * RT, cap=(NT // 128 - 2) * RT),
             pb.counter(qgb, "ob1", RT, 2 * RT, cap=(NT // 128 - 1) * RT)]
    if not prefix_only:
        uctr = [pb.counter(qs, f"ub{tc}", tc * QB + 12 * 65536, 2 * QB,
                           cap=(NT // 128 - 2 + tc) * QB + 12 * 65536)
                for tc in range(2)]

    with pb.loop(NB, key="a0_B" + ("P" if prefix_only else "")) as blk:
        qg.wait(qg)
        qg.wait(dv)
        qg.wait(pe)
        for tc in range(2):
            qg.inc(qg.e.dma_start(
                out=qb[:, tc, :],
                in_=bass.AP(qkvu_ap.tensor, bctr[tc].use(),
                            [[512, 128], [65536, NSPQ], [1, 512]])))
            bctr[tc].adv()
        if not prefix_only:
            qs.wait(qs)
            qs.wait(dv)
            qs.inc(qs.e.dma_start(
                out=stg.ap()[0:128, :, :],
                in_=bass.AP(qkvu_ap.tensor, uctr[0].use(),
                            [[512, 128], [65536, 4], [1, 512]])))
            uctr[0].adv()
            qs.inc(qs.e.dma_start(
                out=rt[:, :],
                in_=bass.AP(qkvu_ap.tensor, uctr[1].use(),
                            [[512, 128], [65536, 4], [1, 512]])))
            uctr[1].adv()
        dv.wait(qg)
        dv.wait(qgb)
        for tc in range(2):
            dv.inc(nc.vector.tensor_copy(
                out=vt16[:, tc, :], in_=qb[:, tc, 4096 + koff:6144 + koff]))
            dv.chain()
            dv.inc(nc.vector.tensor_tensor(
                out=kt16[:, tc, :], in0=qb[:, tc, 2048 + koff:4096 + koff],
                in1=kdecbig[:, tc, :], op=ALU.mult))
            dv.chain()
        for h in range(NH):
            hs = h * HD
            if not prefix_only:
                pe.wait(dv)
                srcs = [qb[:, 0, hs:hs + HD], qb[:, 1, hs:hs + HD],
                        qb[:, 0, 2048 + koff + hs:2048 + koff + hs + HD],
                        qb[:, 1, 2048 + koff + hs:2048 + koff + hs + HD]]
                for j, s in enumerate(srcs):
                    tr = nc.tensor.transpose(
                        ps8[:, 5, j * 128:(j + 1) * 128], s, pb.ident32[:])
                    if j == 3:
                        pe.inc(tr)
                dv.wait(pe)
                dv.inc(nc.vector.tensor_copy(
                    out=qkf16.ap()[0:128, :, :],
                    in_=ps8[:, 5, 0:512].rearrange("p (a b) -> p a b", b=128)))
                dv.chain()
                pe.wait(dv)
                for sc in range(2):
                    pe.inc(nc.tensor.matmul(
                        ps8[:, sc, 0:BLK], qkf16[:, 2 + sc, :],
                        qkf16.ap()[0:128, 0:2, :].rearrange("p a b -> p (a b)"),
                        start=True, stop=True))
                dv.wait(pe)
                for sc in range(2):
                    dv.inc(nc.vector.tensor_tensor(
                        out=sm16[:, sc, :], in0=ps8[:, sc, 0:BLK],
                        in1=maskbig[:, h, sc, :], op=ALU.mult))
                dv.inc(nc.vector.tensor_copy(out=kv16[:, :],
                                             in_=kvst[:, h, :]))
                dv.chain()
                pe.wait(dv)
                for tc in range(2):
                    for sc in range(2):
                        mm = nc.tensor.matmul(
                            ps8[:, 2 + tc, 0:HD], sm16[:, sc,
                                                       tc * 128:(tc + 1) * 128],
                            vt16[:, sc, hs:hs + HD],
                            start=(sc == 0), stop=(sc == 1))
                        if sc == 1:
                            pe.inc(mm)
                    mm = nc.tensor.matmul(
                        ps8[:, 6 + tc, 0:HD], qkf16[:, tc, :], kv16[:, :],
                        start=True, stop=True)
                    pe.inc(mm)
                dv.wait(pe)
                for tc in range(2):
                    dv.inc(nc.vector.tensor_scalar(
                        out=obig[:, tc, hs:hs + HD], in0=ps8[:, 6 + tc, 0:HD],
                        scalar1=qdecc[:, tc, h:h + 1], scalar2=None,
                        op0=ALU.mult))
                    dv.chain()
                    dv.inc(nc.vector.tensor_tensor(
                        out=obig[:, tc, hs:hs + HD],
                        in0=obig[:, tc, hs:hs + HD],
                        in1=ps8[:, 2 + tc, 0:HD], op=ALU.add))
                    dv.chain()
            pe.wait(dv)
            for sc in range(2):
                mm = nc.tensor.matmul(ps8[:, 4, 0:HD], kt16[:, sc, hs:hs + HD],
                                      vt16[:, sc, hs:hs + HD],
                                      start=(sc == 0), stop=(sc == 1))
                if sc == 1:
                    pe.inc(mm)
            dv.wait(pe)
            dv.inc(nc.vector.scalar_tensor_tensor(
                out=kvst[:, h, :], in0=kvst[:, h, :],
                scalar=bdc[:, h:h + 1], in1=ps8[:, 4, 0:HD],
                op0=ALU.mult, op1=ALU.add))
            dv.chain()
        if not prefix_only:
            dv.wait(qs)
            for tc in range(2):
                am = cols[:, 4:5]
                ac.wait(dv)
                ac.inc(nc.scalar.activation(out=tmA[:, :], in_=obig[:, tc, :],
                                            func=AF.Square, accum_out=am))
                ac.chain()
                ac.inc(nc.scalar.activation(out=am, in_=am, func=AF.Sqrt,
                                            scale=1.0 / D, bias=1e-6))
                ac.chain()
                dv.wait(ac)
                dv.inc(nc.vector.reciprocal(out=am, in_=am))
                dv.chain()
                usrc = (stg.ap()[0:128, :, :].rearrange("p a b -> p (a b)")
                        if tc == 0 else rt[:, :])
                dv.inc(nc.vector.scalar_tensor_tensor(
                    out=obig[:, tc, :], in0=obig[:, tc, :], scalar=am,
                    in1=usrc, op0=ALU.mult, op1=ALU.mult))
                dv.chain()
                transpose_f32_to_bf16(
                    pb, lambda c: obig[:, tc, c * 128:(c + 1) * 128],
                    lambda c0, w: ouf.ap()[0:128, c0:c0 + w,
                                           tc * 128:(tc + 1) * 128],
                    NDC, ps8[:, 5, :])
            for sp in range(4):
                par = sp % 2
                qw = qs if par == 0 else qsb
                qw.wait(qw)
                qw.wait(pe)
                qw.inc(qw.e.dma_start(
                    out=wbuf[:, par, :, :],
                    in_=woT_ap[:, :, sp * 512:(sp + 1) * 512]))
                pe.wait(qw)
                pe.wait(dv)
                for tc in range(2):
                    for dc in range(NDC):
                        mm = nc.tensor.matmul(
                            ps8[:, tc * 4 + sp, :],
                            ouf[:, dc, tc * 128:(tc + 1) * 128],
                            wbuf[:, par, dc, :],
                            start=(dc == 0), stop=(dc == NDC - 1))
                        if dc == NDC - 1:
                            pe.inc(mm)
            for tc in range(2):
                qg.wait(qg)
                qg.wait(dv)
                qg.inc(qg.e.dma_start(
                    out=rt[:, :],
                    in_=bass.AP(Rin_ap.tensor, rctrB[tc].use(),
                                [[D, 128], [1, D]])))
                rctrB[tc].adv()
                dv.wait(qg)
                dv.wait(pe)
                dv.wait(qgb)
                for sp in range(4):
                    dv.inc(nc.vector.tensor_tensor(
                        out=obig[:, tc, sp * 512:(sp + 1) * 512],
                        in0=ps8[:, tc * 4 + sp, :],
                        in1=rt[:, sp * 512:(sp + 1) * 512], op=ALU.add))
                    dv.chain()
                qgb.wait(dv)
                qgb.inc(qgb.e.dma_start(
                    out=bass.AP(Rout_ap.tensor, octrB[tc].use(),
                                [[D, 128], [1, D]]),
                    in_=obig[:, tc, :]))
                octrB[tc].adv()
    qg.wait(dv)
    qg.inc(qg.e.dma_start(out=kvend_ap[:, :],
                          in_=kvst.ap()[0:128, :, :].rearrange(
                              "p h e -> p (h e)")))
    pb.bar()


def build_l0attn(registry=None, prefix_only=False, parts=None):
    pb = PB(registry)
    nc = pb.nc
    pb.onesrow = pb.sb("g_ones_r", [1, 128], F32)
    pb.onescol32 = pb.sb("g_ones_c", [128, 1], F32)
    Rin = nc.dram_tensor("Rin", [NT, D], F32, kind="ExternalInput").ap()
    Wqk = nc.dram_tensor("W_qkvu", [8192, D], F32, kind="ExternalInput").ap()
    Wout = nc.dram_tensor("W_out", [D, D], F32, kind="ExternalInput").ap()
    slope = nc.dram_tensor("slope_rate", [16, 1, 1], F32,
                           kind="ExternalInput").ap()
    kvmid = nc.dram_tensor("kvmid", [128, NHD * 128], F32,
                           kind="ExternalInput").ap()
    Rout = nc.dram_tensor("Rout", [NT, D], F32, kind="ExternalOutput").ap()
    kvend = nc.dram_tensor("kvend", [128, NHD * 128], F32,
                           kind="ExternalOutput").ap()
    SW = 4096 if prefix_only else 8192
    qkvu = nc.dram_tensor("qkvu", [NT // 128, SW // 512, 128, 512], F32,
                          kind="Internal").ap()
    wqkT = nc.dram_tensor("wqkT", [128, NDC, 8192], BF16, kind="Internal").ap()
    woT = nc.dram_tensor("woT", [128, NDC, D], BF16, kind="Internal").ap()

    setup_common(pb)
    gp = pb.gp
    gp.inc(nc.gpsimd.memset(pb.onesrow[:, :], 1.0))
    gp.inc(nc.gpsimd.memset(pb.onescol32[:, :], 1.0))
    pb.bar()
    if parts is None or "wq" in parts:
        emit_prep_rowquant(pb, Wqk, wqkT, 8192, D, plain_groups(8192),
                           key="pq", rowquant=False)
    if (parts is None or "wo" in parts) and not prefix_only:
        emit_prep_rowquant(pb, Wout, woT, D, D, plain_groups(D),
                           key="po", rowquant=False)
    if parts is None or "main" in parts:
        emit_l0attn(pb, Rin, Rout, slope, wqkT, woT, kvmid, kvend, qkvu,
                    prefix_only=prefix_only)
    else:
        pb.qg.inc(nc.gpsimd.dma_start(out=Rout[0:128, :], in_=Rin[0:128, :]))
        pb.qg.inc(nc.gpsimd.dma_start(out=kvend[:, :], in_=kvmid[:, :]))
        pb.bar()
    pb.es.close()
    return pb




def _shard_rows(x):
    out = []
    for c in range(NCORE):
        b, h = c // 2, c % 2
        out.append(np.ascontiguousarray(x[b, h * NT:(h + 1) * NT, :],
                                        dtype=np.float32))
    return out


_BUILD_CACHE = {}


def _get(builder, **kw):
    key = (builder.__name__, tuple(sorted(kw.items())))
    if key not in _BUILD_CACHE:
        _BUILD_CACHE[key] = build_twopass(builder, **kw)
    return _BUILD_CACHE[key]


def kernel(L0_attn_x, L0_attn_slope_rate, L0_attn_W_qkvu, L0_attn_W_out,
           L0_ffn_gate_up_packed, L0_ffn_gate_up_scales, L0_ffn_down_packed,
           L0_ffn_down_scales, L1_attn_Wi, L1_attn_Wf, L1_attn_Wg,
           L1_attn_Wo, L1_attn_norm_i, L1_attn_norm_f, L1_attn_norm_g,
           L1_attn_norm_o, L1_attn_g_norm_w, L1_ffn_w_gate_up,
           L1_ffn_w_down):
    x = np.asarray(L0_attn_x, dtype=np.float32)
    B, S, _ = x.shape
    shards = _shard_rows(x)
    f32 = lambda a: np.ascontiguousarray(np.asarray(a, dtype=np.float32))
    wq = f32(L0_attn_W_qkvu)
    wo0 = f32(L0_attn_W_out)
    sr = f32(L0_attn_slope_rate)
    gup8 = np.ascontiguousarray(
        np.asarray(L0_ffn_gate_up_packed, dtype=np.int32)).view(np.uint8)
    gus = np.ascontiguousarray(np.asarray(L0_ffn_gate_up_scales,
                                          dtype=np.float16))
    dnp8 = np.ascontiguousarray(
        np.asarray(L0_ffn_down_packed, dtype=np.int32)).view(np.uint8)
    dns = np.ascontiguousarray(np.asarray(L0_ffn_down_scales,
                                          dtype=np.float16))
    Wi, Wf, Wg, Wo = (f32(L1_attn_Wi), f32(L1_attn_Wf), f32(L1_attn_Wg),
                      f32(L1_attn_Wo))
    ni, nf, ng, no, gnw = (f32(L1_attn_norm_i), f32(L1_attn_norm_f),
                           f32(L1_attn_norm_g), f32(L1_attn_norm_o),
                           f32(L1_attn_g_norm_w))
    w1 = f32(L1_ffn_w_gate_up)
    w2 = f32(L1_ffn_w_down)
    zkv = np.zeros((128, 2048), dtype=np.float32)
    zh = np.zeros((128, 16), dtype=np.float32)
    cores = list(range(NCORE))

    pb = _get(build_l0attn, prefix_only=True)
    res = run_bass_kernel_spmd(pb.nc, [
        dict(Rin=shards[c], W_qkvu=wq, W_out=wo0, slope_rate=sr, kvmid=zkv)
        for c in cores], core_ids=cores)
    kvend = [res.results[c]["kvend"] for c in cores]

    pb = _get(build_l0attn)
    res = run_bass_kernel_spmd(pb.nc, [
        dict(Rin=shards[c], W_qkvu=wq, W_out=wo0, slope_rate=sr,
             kvmid=(zkv if c % 2 == 0 else np.ascontiguousarray(kvend[c - 1])))
        for c in cores], core_ids=cores)
    R = [res.results[c]["Rout"] for c in cores]

    pb = _get(build_l0ffn)
    res = run_bass_kernel_spmd(pb.nc, [
        dict(Rin=R[c], gu_packed_u8=gup8, gu_scales=gus,
             dn_packed_u8=dnp8, dn_scales=dns)
        for c in cores], core_ids=cores)
    R = [res.results[c]["Rout"] for c in cores]

    pb = _get(build_l1attn, prefix_only=True)
    res = run_bass_kernel_spmd(pb.nc, [
        dict(Rin=R[c], Wi=Wi, Wf=Wf, Wg=Wg, Wo=Wo, norm_i=ni, norm_f=nf,
             norm_g=ng, norm_o=no, g_norm_w=gnw, hmid=zh)
        for c in cores], core_ids=cores)
    hend = [res.results[c]["hend"] for c in cores]

    pb = _get(build_l1attn)
    res = run_bass_kernel_spmd(pb.nc, [
        dict(Rin=R[c], Wi=Wi, Wf=Wf, Wg=Wg, Wo=Wo, norm_i=ni, norm_f=nf,
             norm_g=ng, norm_o=no, g_norm_w=gnw,
             hmid=(zh if c % 2 == 0 else np.ascontiguousarray(hend[c - 1])))
        for c in cores], core_ids=cores)
    R = [res.results[c]["Rout"] for c in cores]

    pb = _get(build_l1ffn)
    res = run_bass_kernel_spmd(pb.nc, [
        dict(Rin=R[c], w_gate_up=w1, w_down=w2)
        for c in cores], core_ids=cores)
    R = [res.results[c]["Rout"] for c in cores]

    out = np.empty((B, S, D), dtype=np.float32)
    for c in range(NCORE):
        b, h = c // 2, c % 2
        out[b, h * NT:(h + 1) * NT, :] = R[c]
    return out
